# version 16
# baseline (speedup 1.0000x reference)
import numpy as np
from contextlib import ExitStack

import concourse.bass as bass
import concourse.mybir as mybir
from concourse.bass_utils import run_bass_kernel_spmd

F32 = mybir.dt.float32
BF16 = mybir.dt.bfloat16
U8 = mybir.dt.uint8
AF = mybir.ActivationFunctionType
ALU = mybir.AluOpType

D = 1024
T = 2048
H = 16
NDC = 8
NT = 16
NG = 4
E = 16
EPS_LN = 1e-5
EPS_Z = 1e-6

ENGS = ("tensor", "vector", "scalar", "gpsimd", "sync")


class Em:
    def __init__(self, nc, sems):
        self.nc = nc
        self.sems = sems
        self.ops = {k: [] for k in ENGS}
        self.ctr = {k: 0 for k in sems}
        self.waited = {k: {} for k in ENGS}
        self.trace = {k: [] for k in ENGS}

    def wait(self, eng, dep):
        if dep is None:
            return
        if isinstance(dep, list):
            for d in dep:
                self.wait(eng, d)
            return
        sem, val = dep
        if val <= 0 or self.waited[eng].get(sem, 0) >= val:
            return
        self.waited[eng][sem] = val
        self.trace[eng].append(("w", sem, val))
        self.ops[eng].append(lambda e, s=self.sems[sem], v=val: e.wait_ge(s, v))

    def emit(self, eng, fn, inc=None, n=1):
        if inc is None:
            self.ops[eng].append(fn)
            return None
        self.ctr[inc] += n
        val = self.ctr[inc]
        self.trace[eng].append(("i", inc, n))
        self.ops[eng].append(
            lambda e, f=fn, s=self.sems[inc], k=n: f(e).then_inc(s, k))
        return (inc, val)

    def check_deadlock(self):
        pos = {k: 0 for k in ENGS}
        cnt = {k: 0 for k in self.ctr}
        progress = True
        while progress:
            progress = False
            for eng in ENGS:
                tr = self.trace[eng]
                while pos[eng] < len(tr):
                    kind, sem, v = tr[pos[eng]]
                    if kind == "w" and cnt[sem] < v:
                        break
                    if kind == "i":
                        cnt[sem] += v
                    pos[eng] += 1
                    progress = True
        stuck = {e: (pos[e], len(self.trace[e]), self.trace[e][pos[e]]
                     if pos[e] < len(self.trace[e]) else None) for e in ENGS
                 if pos[e] < len(self.trace[e])}
        if stuck:
            raise RuntimeError(f"deadlock: {stuck}; cnt={cnt}")

    def pe_now(self):
        return ("pe", self.ctr["pe"])


class Bank:
    def __init__(self, ap):
        self.ap = ap
        self.free = []


def build_program():
    nc = bass.Bass("TRN2", target_bir_lowering=False)

    x_d = nc.dram_tensor("x", [T, D], F32, kind="ExternalInput").ap()
    wqt_d = nc.dram_tensor("wqt", [D, D], F32, kind="ExternalInput").ap()
    wkt_d = nc.dram_tensor("wkt", [D, D], F32, kind="ExternalInput").ap()
    wvt_d = nc.dram_tensor("wvt", [D, D], F32, kind="ExternalInput").ap()
    wot_d = nc.dram_tensor("wot", [D, D], F32, kind="ExternalInput").ap()
    gwt_d = nc.dram_tensor("gwt", [D, E], F32, kind="ExternalInput").ap()
    gT_d = nc.dram_tensor("gateT", [E, D, 512], F32, kind="ExternalInput").ap()
    uT_d = nc.dram_tensor("upT", [E, D, 512], F32, kind="ExternalInput").ap()
    dT_d = nc.dram_tensor("downT", [E, 512, D], F32, kind="ExternalInput").ap()
    bias_cm_d = nc.dram_tensor("bias_cm", [128, 16], F32, kind="ExternalInput").ap()
    brows_d = nc.dram_tensor("brows", [3 * D], F32, kind="ExternalInput").ap()
    ebb_d = nc.dram_tensor("ebb", [128, E], F32, kind="ExternalInput").ap()
    mask_d = nc.dram_tensor("mask", [128, 128], F32, kind="ExternalInput").ap()
    ident_d = nc.dram_tensor("ident", [128, 128], F32, kind="ExternalInput").ap()
    y_d = nc.dram_tensor("y", [T, D], F32, kind="ExternalOutput").ap()

    x_tm = nc.alloc_sbuf_tensor("x_tm", [128, NT * D], F32).ap()

    arena_w = nc.alloc_sbuf_tensor("arena_w", [128, 65536], U8).ap()
    wqt_sb = arena_w[:, 0:16384].bitcast(BF16)
    wkt_sb = arena_w[:, 16384:32768].bitcast(BF16)
    wvt_sb = arena_w[:, 32768:49152].bitcast(BF16)
    wot_sb = arena_w[:, 49152:65536].bitcast(BF16)
    moe_slot = []
    for s in range(2):
        base = s * 24576
        moe_slot.append(dict(
            g=arena_w[:, base:base + 8192].bitcast(BF16),
            u=arena_w[:, base + 8192:base + 16384].bitcast(BF16),
            d=arena_w[:, base + 16384:base + 24576].bitcast(BF16)))
    rbase = 49152

    def rview(off, cols):
        return arena_w[:, rbase + off * 4: rbase + (off + cols) * 4].bitcast(F32)
    escore = rview(0, 256)
    sfc = rview(256, 256)
    tmpsc = rview(512, 256)
    rank_e = rview(768, 256)
    combine = rview(1024, 256)
    gs = rview(1280, 64)
    rank_g = rview(1344, 64)
    gmask = rview(1408, 64)
    ptree = rview(1472, 640)
    cmpb = [rview(2112 + i * 256, 256) for i in range(3)]
    dtree = rview(2880, 224)
    den = rview(3104, 16)
    rden = rview(3120, 16)
    spc = rview(3136, 16)

    arena_h = nc.alloc_sbuf_tensor("arena_h", [128, 32768], U8).ap()
    ln_tm_slot = [arena_h[:, 0:2048].bitcast(BF16),
                  arena_h[:, 2048:4096].bitcast(BF16),
                  arena_h[:, 4096:6144].bitcast(BF16),
                  arena_h[:, 6144:8192].bitcast(BF16)]
    attn_tm_slot = [arena_h[:, 4096:6144].bitcast(BF16),
                    arena_h[:, 6144:8192].bitcast(BF16)]
    h1_fm = arena_h[:, 8192:16384].bitcast(BF16)
    attn_fm = arena_h[:, 16384:24576].bitcast(BF16)
    h2_fm = arena_h[:, 0:32768].bitcast(BF16)
    brows_sb = arena_h[:, 24576:30720].bitcast(BF16)
    A_m = arena_h[:, 30720:31744].bitcast(BF16)
    mask_sb = arena_h[:, 31744:32256].bitcast(F32)
    ones_row = arena_h[:, 32256:32512].bitcast(BF16)
    bias_cm = arena_h[:, 32512:32576].bitcast(F32)
    zr_sb = arena_h[:, 32576:32640].bitcast(F32)

    arena_p = nc.alloc_sbuf_tensor("arena_p", [128, 33024], U8).ap()
    q_grp = arena_p[:, 0:8192].bitcast(BF16)
    k_grp = arena_p[:, 8192:16384].bitcast(BF16)
    ktm_grp = arena_p[:, 16384:24576].bitcast(BF16)
    vaug_grp = arena_p[:, 24576:33024].bitcast(BF16)
    act_slot = [arena_p[:, 0:16384].bitcast(BF16),
                arena_p[:, 16384:32768].bitcast(BF16)]
    ln1_junk = arena_p[:, 0:4096].bitcast(F32)
    ln2_junk = arena_p[:, 16384:20480].bitcast(F32)

    scr = nc.alloc_sbuf_tensor("scr", [128, 8192], U8).ap()
    m_scr = [scr[:, 0:2048].bitcast(F32), scr[:, 4096:6144].bitcast(F32)]
    e_scr = [scr[:, 2048:4096].bitcast(F32), scr[:, 6144:8192].bitcast(F32)]
    mask4 = scr[:, 6144:7168].bitcast(BF16)
    A_m2 = scr[:, 7168:8192].bitcast(BF16)
    sg_scr = [scr[:, 0:2048].bitcast(F32), scr[:, 2048:4096].bitcast(F32)]
    h2_tm_slot0 = scr[:, 4096:6144].bitcast(BF16)

    xtra = nc.alloc_sbuf_tensor("xtra", [128, 2048], U8).ap()
    A_m_w1 = xtra[:, 0:1024].bitcast(BF16)
    A_m2_w1 = xtra[:, 1024:2048].bitcast(BF16)
    h2_tm_slot1 = xtra[:, 0:2048].bitcast(BF16)
    h2_tm_slot2 = scr[:, 6144:8192].bitcast(BF16)
    h2_tm_slot = [h2_tm_slot0, h2_tm_slot1, h2_tm_slot2]

    spc_a = nc.alloc_sbuf_tensor("spc_a", [128, 16], F32).ap()
    S_f32 = nc.alloc_sbuf_tensor("S_f32", [128, 528], F32).ap()
    S_bf = nc.alloc_sbuf_tensor("S_bf", [128, 528], BF16).ap()
    ident_sb = nc.alloc_sbuf_tensor("ident_sb", [128, 128], BF16).ap()
    ebb_sb = nc.alloc_sbuf_tensor("ebb_sb", [128, E], F32).ap()
    gwt_sb = nc.alloc_sbuf_tensor("gwt_sb", [128, NDC * E], BF16).ap()
    stats = nc.alloc_sbuf_tensor("stats", [128, 120], F32).ap()

    fb = [Bank(nc.alloc_psum_tensor(f"pb{i}", [128, 512], F32).ap()) for i in range(6)]
    tb = [Bank(nc.alloc_psum_tensor(f"tbk{i}", [128, 1024], BF16).ap()) for i in range(2)]

    with ExitStack() as stack:
        sems = {}
        for name in ("lx0", "lx1", "lx2", "lx3", "lx4", "lx5", "ldc",
                     "lwa", "lwq", "lwk", "lwv", "lwo", "lw0", "lw1",
                     "pe", "dve", "act", "gp", "st"):
            sems[name] = stack.enter_context(nc.semaphore(name))
        em = Em(nc, sems)

        d_x_dep = [None] * NT
        xv = x_d.rearrange("(tt p) d -> p tt d", p=128)

        xsems = ["lx0", "lx1", "lx2", "lx3", "lx4", "lx5"]
        xsem_i = [0]

        def load_x(t0, ntl):
            sem = xsems[xsem_i[0]]
            xsem_i[0] += 1
            ev = em.emit("sync", lambda e: e.dma_start(
                out=x_tm[:, t0 * D:(t0 + ntl) * D].rearrange("p (tt d) -> p tt d", d=D),
                in_=xv[:, t0:t0 + ntl, :]), inc=sem, n=16)
            for t in range(t0, t0 + ntl):
                d_x_dep[t] = ev
        em.emit("sync", lambda e: e.dma_start(out=mask_sb[:, :], in_=mask_d), inc="ldc", n=16)
        em.emit("sync", lambda e: e.dma_start(out=bias_cm[:, :], in_=bias_cm_d), inc="ldc", n=16)
        load_x(0, 1)
        load_x(1, 1)
        load_x(2, 2)
        for g in range(1, NG):
            load_x(4 * g, 4)
        em.emit("sync", lambda e: e.dma_start(out=ebb_sb[:, :], in_=ebb_d), inc="ldc", n=16)
        d_mask = d_bcm = d_ebb = ("ldc", 48)

        def wload(sem, out_sb, in_ap):
            return em.emit("gpsimd", lambda e, o=out_sb, i=in_ap: e.dma_start(out=o, in_=i),
                           inc=sem, n=16)

        wload("lwa", ident_sb[:, :], ident_d)
        wload("lwa", brows_sb[0:1, :], brows_d[None, :])
        d_ident = d_brows = ("lwa", 32)
        d_wq = wload("lwq", wqt_sb.rearrange("p (c o) -> p c o", c=NDC),
                     wqt_d.rearrange("(c p) o -> p c o", p=128))
        d_wk = wload("lwk", wkt_sb.rearrange("p (c o) -> p c o", c=NDC),
                     wkt_d.rearrange("(c p) o -> p c o", p=128))
        d_wv = wload("lwv", wvt_sb.rearrange("p (c o) -> p c o", c=NDC),
                     wvt_d.rearrange("(c p) o -> p c o", p=128))
        wload("lwo", wot_sb.rearrange("p (c o) -> p c o", c=NDC),
              wot_d.rearrange("(c p) o -> p c o", p=128))
        wload("lwo", gwt_sb.rearrange("p (c o) -> p c o", c=NDC),
              gwt_d.rearrange("(c p) o -> p c o", p=128))
        d_wo = d_gwt = ("lwo", 32)

        em.emit("vector", lambda e: e.memset(ones_row[0:1, :], 1.0))
        em.emit("vector", lambda e: e.memset(S_f32[:, :], 0.0))
        em.emit("vector", lambda e: e.memset(S_bf[:, :], 0.0))
        em.emit("vector", lambda e: e.memset(
            vaug_grp.rearrange("p (g c) -> p g c", c=66)[:, :, 64:65], 1.0))
        d_const = em.emit("vector", lambda e: e.memset(
            vaug_grp.rearrange("p (g c) -> p g c", c=66)[:, :, 65:66], 0.0), inc="dve")

        bank_i = [0]

        def acquire_fb():
            b = fb[bank_i[0] % 6]
            bank_i[0] += 1
            em.wait("tensor", b.free)
            b.free = []
            return b

        junk_last = {}

        def ln_stats(xt, st, junk, x_dep, junk_dep, jkey):
            em.wait("scalar", x_dep)
            em.wait("scalar", junk_dep)
            em.wait("scalar", junk_last.get(jkey))
            ev_c = em.emit("scalar", lambda e, o=junk, i=xt, a=st[:, 0:1]:
                           e.activation(o, i, AF.Copy, accum_out=a), inc="act")
            em.wait("scalar", ev_c)
            ev = em.emit("scalar", lambda e, o=junk, i=xt, a=st[:, 1:2]:
                         e.activation(o, i, AF.Square, accum_out=a), inc="act")
            junk_last[jkey] = ev
            return ev

        smalls_last = {}
        sqrt_last = {}

        def ln_smalls(st, ev_stats, skey):
            em.wait("vector", ev_stats)
            em.wait("vector", smalls_last.get(skey))
            em.emit("vector", lambda e, o=st[:, 2:3], i=st[:, 0:1]:
                    e.tensor_scalar_mul(o, i, 1.0 / D))
            em.emit("vector", lambda e, o=st[:, 3:4], i=st[:, 1:2]:
                    e.tensor_scalar(o, i, 1.0 / D, EPS_LN, op0=ALU.mult, op1=ALU.add))
            em.emit("vector", lambda e, o=st[:, 4:5], i=st[:, 2:3]: e.tensor_mul(o, i, i))
            em.emit("vector", lambda e, o=st[:, 9:10], i=st[:, 0:1]:
                    e.tensor_scalar_mul(o, i, 1.0))
            em.emit("vector", lambda e, o=st[:, 5:6], i=st[:, 3:4], m2=st[:, 4:5]:
                    e.tensor_sub(o, i, m2))
            em.emit("vector", lambda e, o=st[:, 9:10], i=st[:, 0:1]:
                    e.tensor_scalar_mul(o, i, 1.0))
            ev = em.emit("vector", lambda e, o=st[:, 6:7], i=st[:, 5:6]:
                         e.reciprocal(o, i), inc="dve")
            smalls_last[skey] = ev
            return ev

        def ln_sqrt(st, ev_v, skey):
            em.wait("scalar", ev_v)
            em.wait("scalar", sqrt_last.get(skey))
            ev = em.emit("scalar", lambda e, o=st[:, 7:8], i=st[:, 6:7]:
                         e.activation(o, i, AF.Sqrt), inc="act")
            sqrt_last[skey] = ev
            return ev

        def ln_fin(xt, out_ap, st, ev_r, deps_out):
            em.wait("vector", ev_r)
            em.wait("vector", deps_out)
            return em.emit("vector", lambda e, o=out_ap, i=xt, m=st[:, 2:3], r=st[:, 7:8]:
                           e.tensor_scalar(o, i, m, r, op0=ALU.subtract, op1=ALU.mult),
                           inc="dve")

        exp_loads = [None] * E
        slot_dead = [None, None]

        def emit_exp_load(e_):
            s = e_ % 2
            sem = "lw0" if s == 0 else "lw1"
            em.wait("gpsimd", slot_dead[s])
            wload(sem, moe_slot[s]["g"].rearrange("p (c i) -> p c i", c=NDC),
                  gT_d[e_].rearrange("(c p) i -> p c i", p=128))
            wload(sem, moe_slot[s]["u"].rearrange("p (c i) -> p c i", c=NDC),
                  uT_d[e_].rearrange("(c p) i -> p c i", p=128))
            exp_loads[e_] = wload(sem, moe_slot[s]["d"].rearrange("p (c o) -> p c o", c=4),
                                  dT_d[e_].rearrange("(c p) o -> p c o", p=128))

        qk_free = []
        ktm_free = []
        vaug_free = []
        attn_tm_free = [[], []]
        lnslot_free = [[], [], [], []]
        zr_last = [None]
        scr_free = [[], []]
        am_free = [[], []]
        mirror_ev = [None, None]
        x_tile_y1 = [None] * NT
        ln_s_ev = [None] * NT
        gp_a3_final = None

        def tm_tile_to_fm(tm_ap, fm_ap, tl, bslot, ready_dep, evac_eng="vector",
                          bank=None):
            b = bank if bank is not None else tb[bslot]
            bap = b.ap if bank is None else b.ap.bitcast(BF16)
            em.wait("tensor", b.free)
            b.free = []
            em.wait("tensor", d_ident)
            last = None
            for dc in range(NDC):
                em.wait("tensor", ready_dep)
                last = em.emit("tensor", lambda e, o=bap[:, dc * 128:(dc + 1) * 128],
                               i=tm_ap[:, dc * 128:(dc + 1) * 128]:
                               e.transpose(o, i, ident_sb[:, :]),
                               inc="pe" if dc == NDC - 1 else None)
            oap = fm_ap.rearrange("p (c t) -> p c t", c=NDC)[:, :, tl * 128:(tl + 1) * 128]
            iap = bap[:, :].rearrange("p (c w) -> p c w", c=NDC)
            if evac_eng == "vector":
                em.wait("vector", last)
                ev = em.emit("vector", lambda e, o=oap, i=iap:
                             e.tensor_copy(o, i), inc="dve")
            else:
                em.wait("scalar", last)
                ev = em.emit("scalar", lambda e, o=oap, i=iap:
                             e.copy(o, i), inc="act")
            b.free = [ev]
            return last, ev

        ln1_evr = [None] * 4

        def ln1_vs_tile(g, tl):
            tt = 4 * g + tl
            st = stats[:, (tt % 4) * 10:(tt % 4) * 10 + 10]
            xt = x_tm[:, tt * D:(tt + 1) * D]
            ev_st = ln_stats(xt, st, ln1_junk, d_x_dep[tt], qk_free, 'j1')
            ev_v = ln_smalls(st, ev_st, ('a', tt % 4))
            ln1_evr[tl] = ln_sqrt(st, ev_v, ('a', tt % 4))

        def ln1_fin_tile(g, tl):
            tt = 4 * g + tl
            st = stats[:, (tt % 4) * 10:(tt % 4) * 10 + 10]
            xt = x_tm[:, tt * D:(tt + 1) * D]
            guards = lnslot_free[tl] + (attn_tm_free[tl - 2] if tl >= 2 else [])
            ln_s_ev[tt] = ln_fin(xt, ln_tm_slot[tl][:, :], st, ln1_evr[tl], guards)

        def ln1_T_tile(g, tl):
            tt = 4 * g + tl
            pe_last, evac = tm_tile_to_fm(ln_tm_slot[tl], h1_fm, tl, tl % 2,
                                          ln_s_ev[tt])
            lnslot_free[tl] = [pe_last]
            return evac

        ev_s2 = [None] * NT
        h2_free = [[], [], []]
        h2_ready = [None] * NT
        sig_evs = []
        ln2_evr = [None] * NT

        def lnv2_a(tt):
            xt = x_tm[:, tt * D:(tt + 1) * D]
            st = stats[:, 40 + (tt % 8) * 10: 40 + (tt % 8) * 10 + 10]
            ev_st = ln_stats(xt, st, ln2_junk, x_tile_y1[tt], None, 'j2')
            ev_v = ln_smalls(st, ev_st, ('b', tt % 8))
            ln2_evr[tt] = ln_sqrt(st, ev_v, ('b', tt % 8))

        def lnv2_b(tt):
            xt = x_tm[:, tt * D:(tt + 1) * D]
            st = stats[:, 40 + (tt % 8) * 10: 40 + (tt % 8) * 10 + 10]
            extra = [gp_a3_final] if tt % 3 != 1 else []
            ev_s2[tt] = ln_fin(xt, h2_tm_slot[tt % 3][:, :], st, ln2_evr[tt],
                               h2_free[tt % 3] + extra)

        for tl in range(4):
            ln1_vs_tile(0, tl)
            if tl >= 1:
                ln1_fin_tile(0, tl - 1)
        ln1_fin_tile(0, 3)
        h1fm_ready = [ln1_T_tile(0, tl) for tl in range(4)]

        for g in range(NG):
            def phi_evac(bank, last_pe, bias_ap, out_ap, si, out_free):
                em.wait("vector", last_pe)
                em.wait("vector", scr_free[si])
                if bias_ap is not None:
                    ev_m = em.emit("vector", lambda e, o=m_scr[si], i=bank.ap[:, :], ba=bias_ap:
                                   e.tensor_scalar(o, i, ba, 0.0, op0=ALU.add, op1=ALU.min),
                                   inc="dve")
                else:
                    ev_m = em.emit("vector", lambda e, o=m_scr[si], i=bank.ap[:, :]:
                                   e.tensor_scalar_min(o, i, 0.0), inc="dve")
                em.wait("scalar", ev_m)
                ev_e = em.emit("scalar", lambda e, o=e_scr[si], i=m_scr[si]:
                               e.activation(o, i, AF.Exp), inc="act")
                em.wait("scalar", last_pe)
                if bias_ap is not None:
                    em.wait("scalar", d_bcm)
                    ev_r = em.emit("scalar", lambda e, o=m_scr[si], i=bank.ap[:, :], ba=bias_ap:
                                   e.activation(o, i, AF.Relu, bias=ba), inc="act")
                else:
                    ev_r = em.emit("scalar", lambda e, o=m_scr[si], i=bank.ap[:, :]:
                                   e.activation(o, i, AF.Relu), inc="act")
                bank.free = [ev_r]
                em.wait("gpsimd", ev_e)
                em.wait("gpsimd", ev_r)
                em.wait("gpsimd", out_free)
                ev = em.emit("gpsimd", lambda e, o=out_ap, a=e_scr[si], bb=m_scr[si]:
                             e.tensor_add(o, a, bb), inc="gp")
                scr_free[si] = [ev]
                return ev

            q_ready, k_ready = [], []
            for which, w_sb, w_dep, bcol, out_grp, rdy in (
                    ("q", wqt_sb, d_wq, 0, q_grp, q_ready),
                    ("k", wkt_sb, d_wk, 8, k_grp, k_ready)):
                for oc in range(NDC):
                    b = acquire_fb()
                    em.wait("tensor", w_dep)
                    em.wait("tensor", h1fm_ready)
                    last = None
                    for dc in range(NDC):
                        last = em.emit("tensor", lambda e, o=b.ap[:, :],
                                       l=w_sb[:, dc * 1024 + oc * 128: dc * 1024 + oc * 128 + 128],
                                       r=h1_fm[:, dc * 512:(dc + 1) * 512],
                                       s=(dc == 0), p=(dc == NDC - 1):
                                       e.matmul(o, l, r, start=s, stop=p),
                                       inc="pe" if dc == NDC - 1 else None)
                    em.wait("vector", d_bcm)
                    bap = bias_cm[:, bcol + oc: bcol + oc + 1]
                    ev = phi_evac(b, last, bap, out_grp[:, oc * 512:(oc + 1) * 512],
                                  oc % 2, qk_free)
                    rdy.append(ev)
            qk_free = []
            if g == NG - 1:
                slot_dead[0] = em.pe_now()
                emit_exp_load(0)

            v_ready = []
            for tl in range(4):
                evs_tl = []
                for dh in range(2):
                    b = acquire_fb()
                    em.wait("tensor", [d_wv, d_brows])
                    em.wait("tensor", h1fm_ready)
                    for dc in range(NDC):
                        em.emit("tensor", lambda e, o=b.ap[:, :],
                                l=h1_fm[:, dc * 512 + tl * 128: dc * 512 + tl * 128 + 128],
                                r=wvt_sb[:, dc * 1024 + dh * 512: dc * 1024 + dh * 512 + 512],
                                s=(dc == 0): e.matmul(o, l, r, start=s, stop=False))
                    last = em.emit("tensor", lambda e, o=b.ap[:, :],
                                   r=brows_sb[0:1, D + dh * 512: D + dh * 512 + 512]:
                                   e.matmul(o, ones_row[0:1, :], r, start=False, stop=True),
                                   inc="pe")
                    em.wait("vector", last)
                    em.wait("vector", vaug_free)
                    ev = em.emit("vector", lambda e,
                                 o=vaug_grp[:, tl * 1056 + 528 * dh: tl * 1056 + 528 * dh + 528]
                                 .rearrange("p (h c) -> p h c", c=66)[:, :, 0:64],
                                 i=b.ap[:, :].rearrange("p (h c) -> p h c", c=64):
                                 e.tensor_copy(o, i), inc="dve")
                    b.free = [ev]
                    evs_tl.append(ev)
                v_ready.append(evs_tl)
            vaug_free = []
            if g == NG - 1:
                slot_dead[1] = em.pe_now()
                emit_exp_load(1)
                gp_a3_final = ("gp", em.ctr["gp"])

            ktm_ready = []
            for tl in range(4):
                b = tb[tl % 2]
                em.wait("tensor", b.free)
                b.free = []
                em.wait("tensor", d_ident)
                em.wait("tensor", k_ready)
                last = None
                for oc in range(NDC):
                    last = em.emit("tensor", lambda e, o=b.ap[:, oc * 128:(oc + 1) * 128],
                                   i=k_grp[:, oc * 512 + tl * 128: oc * 512 + tl * 128 + 128]:
                                   e.transpose(o, i, ident_sb[:, :]),
                                   inc="pe" if oc == NDC - 1 else None)
                em.wait("vector", last)
                em.wait("vector", ktm_free)
                ev = em.emit("vector", lambda e,
                             o=ktm_grp[:, tl * 1024:(tl + 1) * 1024],
                             i=b.ap[:, :]: e.tensor_copy(o, i), inc="dve")
                b.free = [ev]
                ktm_ready.append([ev])
            ktm_free = []

            em.wait("vector", q_ready + k_ready)
            em.wait("vector", d_mask)
            for c4 in range(4):
                em.emit("vector", lambda e, o=mask4[:, c4 * 128:(c4 + 1) * 128],
                        i=mask_sb[:, :]: e.tensor_copy(o, i))

            bA = [fb[0], fb[1]]
            bO0 = [fb[2], fb[3]]
            bS = [fb[4], fb[5]]
            am_bufs = [(A_m, A_m2), (A_m_w1, A_m2_w1)]
            at_ready = [None] * 4
            attn_done_last = [None] * 4

            def emit_A6(cl):
                pe_last, evac = tm_tile_to_fm(attn_tm_slot[cl % 2], attn_fm, cl,
                                              None, attn_done_last[cl],
                                              bank=bS[cl % 2])
                attn_tm_free[cl % 2] = [pe_last]
                at_ready[cl] = evac

            for cl in range(4):
                if cl > 0:
                    emit_A6(cl - 1)
                wbanks = [(bA[0], bA[1]), (tb[0], tb[1])]
                waps = []
                ev_a = []
                for w in range(2):
                    b0, b1 = wbanks[w]
                    a0 = b0.ap if w == 0 else b0.ap.bitcast(F32)
                    a1 = b1.ap if w == 0 else b1.ap.bitcast(F32)
                    waps.append((a0, a1))
                    em.wait("tensor", b0.free + b1.free)
                    b0.free = []
                    b1.free = []
                    eve = evo = None
                    for jj in range(4):
                        j = 4 * w + jj
                        win = j * 512 + cl * 128
                        em.wait("tensor", [q_ready[j], k_ready[j]])
                        eve = em.emit("tensor", lambda e, o=a0[:, jj * 128:jj * 128 + 128],
                                      l=k_grp[0:64, win:win + 128],
                                      r=q_grp[0:64, win:win + 128],
                                      s=(jj == 0), p=(jj == 3):
                                      e.matmul(o, l, r, start=s, stop=p),
                                      inc="pe" if jj == 3 else None)
                        evo = em.emit("tensor", lambda e, o=a1[:, jj * 128:jj * 128 + 128],
                                      l=k_grp[64:128, win:win + 128],
                                      r=q_grp[64:128, win:win + 128],
                                      s=(jj == 0), p=(jj == 3):
                                      e.matmul(o, l, r, start=s, stop=p,
                                               tile_position=(64, 0)),
                                      inc="pe" if jj == 3 else None)
                    ev_a.append([eve, evo])

                ev_me = [None, None]
                ev_mo = [None, None]
                for w in range(2):
                    a0, a1 = waps[w]
                    ame, amo = am_bufs[w]
                    em.wait("vector", ev_a[w])
                    em.wait("vector", am_free[w])
                    ev_me[w] = em.emit("vector", lambda e, i=a0, o=ame:
                                       e.tensor_mul(o[:, :], i, mask4[:, :]), inc="dve")
                    ev_mo[w] = em.emit("vector", lambda e, i=a1, o=amo:
                                       e.tensor_mul(o[:, :], i, mask4[:, :]), inc="dve")
                    if w == 0:
                        bA[0].free = [ev_me[0]]
                        bA[1].free = [ev_mo[0]]

                for w in range(2):
                    ame, amo = am_bufs[w]
                    av_done = None
                    if w == 0:
                        bo0, bo1 = bO0[0], bO0[1]
                        o0, o1 = bo0.ap, bo1.ap
                    else:
                        bo0, bo1 = tb[0], tb[1]
                        o0, o1 = bo0.ap.bitcast(F32), bo1.ap.bitcast(F32)

                    def vap_of(h):
                        return vaug_grp[:, cl * 1056 + 66 * h: cl * 1056 + 66 * h + 66]

                    if w == 0:
                        em.wait("tensor", bo0.free + bo1.free)
                        bo0.free = []
                        bo1.free = []
                    em.wait("tensor", ev_me[w])
                    em.wait("tensor", v_ready[cl] + ktm_ready[cl])
                    for jj in range(4):
                        em.emit("tensor", lambda e, o=o0[:, 66 * jj:66 * jj + 66],
                                l=ame[:, jj * 128:jj * 128 + 128],
                                r=vap_of(2 * (4 * w + jj)), s=(jj == 0):
                                e.matmul(o, l, r, start=s, stop=False))
                    em.wait("tensor", ev_mo[w])
                    for jj in range(4):
                        av_done = em.emit("tensor", lambda e, o=o1[:, 66 * jj:66 * jj + 66],
                                l=amo[:, jj * 128:jj * 128 + 128],
                                r=vap_of(2 * (4 * w + jj) + 1), s=(jj == 0):
                                e.matmul(o, l, r, start=s, stop=False),
                                inc="pe" if jj == 3 else None)
                    am_free[w] = [av_done]
                    em.wait("tensor", mirror_ev[w] if mirror_ev[w] else d_const)
                    for jj in range(4):
                        j = 4 * w + jj
                        win = j * 512 + cl * 128
                        em.emit("tensor", lambda e, o=o0[:, 66 * jj:66 * jj + 66],
                                l=q_grp[0:64, win:win + 128],
                                r=S_bf[0:64, 66 * j:66 * j + 66], p=(jj == 3):
                                e.matmul(o, l, r, start=False, stop=p))
                    ev_qs = None
                    for jj in range(4):
                        j = 4 * w + jj
                        win = j * 512 + cl * 128
                        ev_qs = em.emit("tensor", lambda e, o=o1[:, 66 * jj:66 * jj + 66],
                                l=q_grp[64:128, win:win + 128],
                                r=S_bf[64:128, 66 * j:66 * j + 66], p=(jj == 3):
                                e.matmul(o, l, r, start=False, stop=p,
                                         tile_position=(64, 0)),
                                inc="pe" if jj == 3 else None)
                    em.wait("tensor", bS[0].free + bS[1].free)
                    bS[0].free = []
                    bS[1].free = []
                    pe_wave_last = None
                    for jj in range(4):
                        em.emit("tensor", lambda e, o=bS[0].ap[0:64, 66 * jj:66 * jj + 66],
                                l=ktm_grp[:, cl * 1024 + 128 * (4 * w + jj):
                                          cl * 1024 + 128 * (4 * w + jj) + 64],
                                r=vap_of(2 * (4 * w + jj)), s=(jj == 0), p=(jj == 3):
                                e.matmul(o, l, r, start=s, stop=p))
                        pe_wave_last = em.emit("tensor", lambda e,
                                o=bS[1].ap[0:64, 66 * jj:66 * jj + 66],
                                l=ktm_grp[:, cl * 1024 + 128 * (4 * w + jj) + 64:
                                          cl * 1024 + 128 * (4 * w + jj) + 128],
                                r=vap_of(2 * (4 * w + jj) + 1), s=(jj == 0), p=(jj == 3):
                                e.matmul(o, l, r, start=s, stop=p), inc="pe")
                    em.wait("vector", ev_qs)
                    for par, oo in ((0, o0), (1, o1)):
                        zin = oo[:, 0:264].rearrange("p (j c) -> p j c", c=66)[:, :, 64:65]
                        em.emit("vector", lambda e,
                                o=zr_sb[:, 8 * par:8 * par + 4].rearrange("p (j o) -> p j o", o=1),
                                i=zin: e.tensor_scalar_add(o, i, EPS_Z))
                    for par in range(2):
                        em.emit("vector", lambda e, o=zr_sb[:, 8 * par + 4:8 * par + 8],
                                i=zr_sb[:, 8 * par:8 * par + 4]: e.reciprocal(o, i))
                    em.wait("vector", zr_last[0])
                    zr_ev = em.emit("vector", lambda e: e.tensor_scalar_mul(
                        spc_a[:, 0:8], zr_sb[:, 0:8], 1.0), inc="dve")
                    zr_last[0] = zr_ev
                    em.wait("scalar", zr_ev)
                    em.wait("scalar", attn_tm_free[cl % 2])
                    evs0 = []
                    for jj in range(4):
                        h = 8 * w + 2 * jj
                        evv = em.emit("scalar", lambda e,
                                      o=attn_tm_slot[cl % 2][:, 64 * h: 64 * h + 64],
                                      i=o0[:, 66 * jj:66 * jj + 64],
                                      r=zr_sb[:, 4 + jj:5 + jj]:
                                      e.activation(o, i, AF.Copy, scale=r), inc="act")
                        evs0.append(evv)
                    bo0.free = evs0
                    em.wait("vector", attn_tm_free[cl % 2])
                    evs1 = []
                    for jj in range(4):
                        h = 8 * w + 2 * jj + 1
                        evv = em.emit("vector", lambda e,
                                      o=attn_tm_slot[cl % 2][:, 64 * h: 64 * h + 64],
                                      i=o1[:, 66 * jj:66 * jj + 64],
                                      r=zr_sb[:, 12 + jj:13 + jj]:
                                      e.tensor_scalar_mul(o, i, r), inc="dve")
                        evs1.append(evv)
                    bo1.free = evs1
                    attn_done_last[cl] = [evs0[-1], evs1[-1]]
                    scol = 264 * w
                    em.wait("vector", pe_wave_last)
                    em.emit("vector", lambda e, o=S_f32[0:64, scol:scol + 264],
                            i=bS[0].ap[0:64, 0:264]: e.tensor_add(o, o, i))
                    ev_su = em.emit("vector", lambda e, o=S_f32[64:128, scol:scol + 264],
                                    i=bS[1].ap[0:64, 0:264]: e.tensor_add(o, o, i), inc="dve")
                    bS[0].free = [ev_su]
                    bS[1].free = [ev_su]
                    em.wait("scalar", ev_su)
                    mirror_ev[w] = em.emit("scalar", lambda e, o=S_bf[:, scol:scol + 264],
                                           i=S_f32[:, scol:scol + 264]: e.copy(o, i), inc="act")
            qk_free = [em.pe_now()]
            scr_free[1] = scr_free[1] + [em.pe_now()]
            ktm_free = [em.pe_now()]
            vaug_free = [em.pe_now()]

            nxt = g + 1 < NG
            nready = []
            for tl in range(4):
                if tl == 2:
                    emit_A6(3)
                tt = 4 * g + tl
                for dh in range(2):
                    b = acquire_fb()
                    em.wait("tensor", [d_wo, d_brows])
                    em.wait("tensor", at_ready[tl])
                    for dc in range(NDC):
                        em.emit("tensor", lambda e, o=b.ap[:, :],
                                l=attn_fm[:, dc * 512 + tl * 128: dc * 512 + tl * 128 + 128],
                                r=wot_sb[:, dc * 1024 + dh * 512: dc * 1024 + dh * 512 + 512],
                                s=(dc == 0): e.matmul(o, l, r, start=s, stop=False))
                    last = em.emit("tensor", lambda e, o=b.ap[:, :],
                                   r=brows_sb[0:1, 2 * D + dh * 512: 2 * D + dh * 512 + 512]:
                                   e.matmul(o, ones_row[0:1, :], r, start=False, stop=True),
                                   inc="pe")
                    em.wait("vector", last)
                    ev = em.emit("vector", lambda e,
                                 o=x_tm[:, tt * D + dh * 512: tt * D + dh * 512 + 512],
                                 i=b.ap[:, :]:
                                 e.scalar_tensor_tensor(o, i, 1.0, o, op0=ALU.mult, op1=ALU.add),
                                 inc="dve")
                    b.free = [ev]
                    x_tile_y1[tt] = ev
                if nxt:
                    ln1_vs_tile(g + 1, tl)
                    if tl >= 1:
                        ln1_fin_tile(g + 1, tl - 1)
                        nready.append(ln1_T_tile(g + 1, tl - 1))
                else:
                    if tl <= 2:
                        lnv2_a(tl)
                    if tl >= 2:
                        lnv2_b(tl - 2)
            if nxt:
                ln1_fin_tile(g + 1, 3)
                nready.append(ln1_T_tile(g + 1, 2))
                nready.append(ln1_T_tile(g + 1, 3))
                h1fm_ready = nready

        act_free = [[], []]
        sg_free = [[], []]
        act_evs_by_e = [dict() for _ in range(E)]
        x_final = [None] * NT

        def emit_gu(e_, tc):
            s = e_ % 2
            aslot = act_slot[s]
            for it in range(4):
                bg = acquire_fb()
                em.wait("tensor", exp_loads[e_])
                em.wait("tensor", [h2_ready[4 * tc + i] for i in range(4)])
                ev_g = None
                for dc in range(NDC):
                    ev_g = em.emit("tensor", lambda e, o=bg.ap[:, :],
                                   l=moe_slot[s]["g"][:, dc * 512 + it * 128: dc * 512 + it * 128 + 128],
                                   r=h2_fm[:, dc * 2048 + tc * 512: dc * 2048 + tc * 512 + 512],
                                   st=(dc == 0), sp=(dc == NDC - 1):
                                   e.matmul(o, l, r, start=st, stop=sp),
                                   inc="pe" if dc == NDC - 1 else None)
                bu = acquire_fb()
                ev_u = None
                for dc in range(NDC):
                    ev_u = em.emit("tensor", lambda e, o=bu.ap[:, :],
                                   l=moe_slot[s]["u"][:, dc * 512 + it * 128: dc * 512 + it * 128 + 128],
                                   r=h2_fm[:, dc * 2048 + tc * 512: dc * 2048 + tc * 512 + 512],
                                   st=(dc == 0), sp=(dc == NDC - 1):
                                   e.matmul(o, l, r, start=st, stop=sp),
                                   inc="pe" if dc == NDC - 1 else None)
                si = (tc * 4 + it) % 2
                em.wait("scalar", ev_g)
                em.wait("scalar", sg_free[si])
                em.wait("scalar", gp_a3_final)
                ev_sg = em.emit("scalar", lambda e, o=sg_scr[si], i=bg.ap[:, :]:
                                e.activation(o, i, AF.Silu), inc="act")
                bg.free = [ev_sg]
                em.wait("vector", [ev_u, ev_sg])
                em.wait("vector", act_free[s])
                ev_a = em.emit("vector", lambda e,
                               o=aslot[:, it * 2048 + tc * 512: it * 2048 + tc * 512 + 512],
                               i=bu.ap[:, :], sg=sg_scr[si]:
                               e.scalar_tensor_tensor(o, i, 1.0, sg, op0=ALU.mult, op1=ALU.mult),
                               inc="dve")
                bu.free = [ev_a]
                sg_free[si] = [ev_a]
                act_evs_by_e[e_][(tc, it)] = ev_a

        def emit_down(e_, comb_ev):
            s = e_ % 2
            aslot = act_slot[s]
            for tt in range(NT):
                tc = tt // 4
                for dh in range(2):
                    bd = acquire_fb()
                    em.wait("tensor", [act_evs_by_e[e_][(tc, it)] for it in range(4)])
                    last = None
                    for it in range(4):
                        last = em.emit("tensor", lambda e, o=bd.ap[:, :],
                                       l=aslot[:, it * 2048 + tt * 128: it * 2048 + tt * 128 + 128],
                                       r=moe_slot[s]["d"][:, it * 1024 + dh * 512: it * 1024 + dh * 512 + 512],
                                       st=(it == 0), sp=(it == 3):
                                       e.matmul(o, l, r, start=st, stop=sp),
                                       inc="pe" if it == 3 else None)
                    em.wait("vector", last)
                    em.wait("vector", comb_ev)
                    ev = em.emit("vector", lambda e,
                                 o=x_tm[:, tt * D + dh * 512: tt * D + dh * 512 + 512],
                                 i=bd.ap[:, :], c=combine[:, e_ * 16 + tt: e_ * 16 + tt + 1]:
                                 e.scalar_tensor_tensor(o, i, c, o, op0=ALU.mult, op1=ALU.add),
                                 inc="dve")
                    bd.free = [ev]
                    if e_ == E - 1:
                        x_final[tt] = ev
            act_free[s] = [em.pe_now()]
            slot_dead[s] = em.pe_now()
            if e_ + 2 < E:
                emit_exp_load(e_ + 2)

        def do_T2(tt):
            b = tb[tt % 2]
            em.wait("tensor", b.free)
            b.free = []
            em.wait("tensor", ev_s2[tt])
            last = None
            for dc in range(NDC):
                last = em.emit("tensor", lambda e, o=b.ap[:, dc * 128:(dc + 1) * 128],
                               i=h2_tm_slot[tt % 3][:, dc * 128:(dc + 1) * 128]:
                               e.transpose(o, i, ident_sb[:, :]),
                               inc="pe" if dc == NDC - 1 else None)
            h2_free[tt % 3] = [last]
            em.wait("vector", last)
            ev = em.emit("vector", lambda e,
                         o=h2_fm.rearrange("p (c t) -> p c t", c=NDC)[:, :, tt * 128:(tt + 1) * 128],
                         i=b.ap[:, :].rearrange("p (c w) -> p c w", c=NDC):
                         e.tensor_copy(o, i), inc="dve")
            h2_ready[tt] = ev
            b.free = [ev]

        def do_router(tt):
            brt = acquire_fb()
            em.wait("tensor", d_gwt)
            em.wait("tensor", h2_ready[tt])
            rlast = None
            for dc in range(NDC):
                rlast = em.emit("tensor", lambda e, o=brt.ap[:, 0:16],
                                l=h2_fm[:, dc * 2048 + tt * 128: dc * 2048 + tt * 128 + 128],
                                r=gwt_sb[:, dc * E:(dc + 1) * E],
                                s=(dc == 0), p=(dc == NDC - 1):
                                e.matmul(o, l, r, start=s, stop=p),
                                inc="pe" if dc == NDC - 1 else None)
            em.wait("scalar", rlast)
            evsig = em.emit("scalar", lambda e,
                            o=escore.rearrange("p (ee t) -> p ee t", ee=E)[:, :, tt:tt + 1],
                            i=brt.ap[:, 0:16].rearrange("p (a ee) -> p ee a", a=1):
                            e.activation(o, i, AF.Sigmoid), inc="act")
            sig_evs.append(evsig)
            brt.free = [evsig]

        for tc in range(4):
            for tl in range(4):
                tt = 4 * tc + tl
                do_T2(tt)
                if tt + 3 < NT:
                    lnv2_a(tt + 3)
                if tt + 2 < NT:
                    lnv2_b(tt + 2)
                do_router(tt)
            emit_gu(0, tc)

        def eb(t, e_):
            return t[:, e_ * 16:(e_ + 1) * 16]

        sfc3 = sfc.rearrange("p (g f) -> p g f", g=4)
        pt3 = ptree.rearrange("p (g f) -> p g f", g=4)
        spacer = lambda e: e.tensor_scalar_mul(spc, gs[:, 0:16], 1.0)
        comb_ev_box = [None]

        def emit_ge(d):
            w = (16 - d) * 16
            em.emit("vector", lambda e, o=cmpb[d % 3], x1=tmpsc[:, 0:w],
                    x2=tmpsc[:, d * 16:d * 16 + w], w=w:
                    e.tensor_tensor(o[:, 0:w], x1, x2, op=ALU.is_ge))

        def emit_rank_e(d_lo, d_hi):
            for d in range(d_lo, d_hi):
                w = (16 - d) * 16
                em.emit("vector", lambda e, x=rank_e[:, d * 16:d * 16 + w],
                        c=cmpb[d % 3][:, 0:w]: e.tensor_add(x, x, c))
                if d + 2 <= 15:
                    emit_ge(d + 2)
                else:
                    em.emit("vector", spacer)
                em.emit("vector", lambda e, x=rank_e[:, 0:w],
                        c=cmpb[d % 3][:, 0:w]: e.tensor_sub(x, x, c))
                em.emit("vector", spacer)

        def emit_tree_seg(seg):
            if seg == 0:
                em.wait("vector", sig_evs)
                em.wait("vector", d_ebb)
                for e_ in range(E):
                    em.emit("vector", lambda e, o=eb(sfc, e_), i=eb(escore, e_),
                            c=ebb_sb[:, e_:e_ + 1]: e.tensor_scalar_add(o, i, c))
                pairs4 = [(0, 1), (0, 2), (0, 3), (1, 2), (1, 3), (2, 3)]
                for pi, (a1, a2) in enumerate(pairs4):
                    em.emit("vector", lambda e, o=pt3[:, :, pi * 16:(pi + 1) * 16],
                            x1=sfc3[:, :, a1 * 16:(a1 + 1) * 16],
                            x2=sfc3[:, :, a2 * 16:(a2 + 1) * 16]:
                            e.tensor_add(o, x1, x2))
                for s1, s2, dst in ((0, 1, 6), (2, 3, 7), (4, 5, 8)):
                    em.emit("vector", lambda e, o=pt3[:, :, dst * 16:(dst + 1) * 16],
                            x1=pt3[:, :, s1 * 16:(s1 + 1) * 16],
                            x2=pt3[:, :, s2 * 16:(s2 + 1) * 16]:
                            e.tensor_max(o, x1, x2))
                em.emit("vector", spacer)
                em.emit("vector", lambda e, o=pt3[:, :, 144:160], x1=pt3[:, :, 96:112],
                        x2=pt3[:, :, 112:128]: e.tensor_max(o, x1, x2))
                em.emit("vector", spacer)
                em.emit("vector", lambda e, o=gs.rearrange("p (g f) -> p g f", g=4),
                        x1=pt3[:, :, 144:160], x2=pt3[:, :, 128:144]:
                        e.tensor_max(o, x1, x2))
                for gi in range(4):
                    em.emit("vector", lambda e, o=rank_g[:, gi * 16:(gi + 1) * 16],
                            v=float(3 - gi): e.memset(o, v))
            elif seg == 1:
                gcmp = [cmpb[i][:, 0:48] for i in range(3)]
                for d in range(1, 4):
                    w = (4 - d) * 16
                    em.emit("vector", lambda e, o=gcmp[d - 1][:, 0:w], x1=gs[:, 0:w],
                            x2=gs[:, d * 16:d * 16 + w], w=w:
                            e.tensor_tensor(o[:, 0:w], x1, x2, op=ALU.is_ge))
                for d in range(1, 4):
                    w = (4 - d) * 16
                    em.emit("vector", lambda e, x=rank_g[:, d * 16:d * 16 + w],
                            c=gcmp[d - 1][:, 0:w]: e.tensor_add(x, x, c))
                    em.emit("vector", spacer)
                    em.emit("vector", lambda e, x=rank_g[:, 0:w],
                            c=gcmp[d - 1][:, 0:w]: e.tensor_sub(x, x, c))
                    em.emit("vector", spacer)
                em.emit("vector", lambda e: e.tensor_single_scalar(gmask, rank_g, 2.0, ALU.is_lt))
                em.emit("vector", spacer)
                gm3 = gmask.rearrange("p (g f) -> p g f", g=4)
                tm3 = tmpsc.rearrange("p (g f) -> p g f", g=4)
                for a in range(4):
                    em.emit("vector", lambda e, o=tm3[:, :, a * 16:(a + 1) * 16],
                            x1=sfc3[:, :, a * 16:(a + 1) * 16], x2=gm3:
                            e.tensor_mul(o, x1, x2))
                for e_ in range(E):
                    em.emit("vector", lambda e, o=eb(rank_e, e_), v=float(15 - e_):
                            e.memset(o, v))
                emit_ge(1)
                emit_ge(2)
                emit_rank_e(1, 5)
            elif seg == 2:
                emit_rank_e(5, 11)
            else:
                emit_rank_e(11, 16)
                em.emit("vector", lambda e: e.tensor_single_scalar(sfc, rank_e, 4.0, ALU.is_lt))
                em.emit("vector", spacer)
                em.emit("vector", lambda e: e.tensor_mul(tmpsc, escore, sfc))
                em.emit("vector", spacer)
                tm8 = tmpsc.rearrange("p (h f) -> p h f", h=8)
                dt8 = dtree[:, 0:128].rearrange("p (h f) -> p h f", h=8)
                em.emit("vector", lambda e: e.tensor_add(dt8, tm8[:, :, 0:16], tm8[:, :, 16:32]))
                em.emit("vector", spacer)
                dt4i = dtree[:, 0:128].rearrange("p (h f) -> p h f", h=4)
                dt4 = dtree[:, 128:192].rearrange("p (h f) -> p h f", h=4)
                em.emit("vector", lambda e: e.tensor_add(dt4, dt4i[:, :, 0:16], dt4i[:, :, 16:32]))
                em.emit("vector", spacer)
                dt2i = dtree[:, 128:192].rearrange("p (h f) -> p h f", h=2)
                em.emit("vector", lambda e, o=dtree[:, 192:224].rearrange("p (h f) -> p h f", h=2):
                        e.tensor_add(o, dt2i[:, :, 0:16], dt2i[:, :, 16:32]))
                em.emit("vector", spacer)
                em.emit("vector", lambda e: e.tensor_add(den, dtree[:, 192:208], dtree[:, 208:224]))
                em.emit("vector", spacer)
                em.emit("vector", lambda e: e.tensor_scalar_add(den, den, 1e-20))
                em.emit("vector", spacer)
                em.emit("vector", lambda e: e.reciprocal(rden, den))
                em.emit("vector", spacer)
                for e_ in range(E):
                    comb_ev_box[0] = em.emit("vector",
                                             lambda e, o=eb(combine, e_), i=eb(tmpsc, e_):
                                             e.tensor_mul(o, i, rden), inc="dve")

        for tc in range(4):
            emit_gu(1, tc)
            emit_tree_seg(tc)
        for k in range(E // 2):
            if k > 0:
                for tc in range(4):
                    emit_gu(2 * k, tc)
                for tc in range(4):
                    emit_gu(2 * k + 1, tc)
            emit_down(2 * k, comb_ev_box[0])
            emit_down(2 * k + 1, comb_ev_box[0])

        yv = y_d.rearrange("(tt p) d -> p tt d", p=128)
        for tt in range(NT):
            qeng = "sync" if tt % 2 == 0 else "gpsimd"
            em.wait(qeng, x_final[tt])
            em.emit(qeng, lambda e, o=yv[:, tt:tt + 1, :],
                    i=x_tm[:, tt * D:(tt + 1) * D].rearrange("p (a d) -> p a d", a=1):
                    e.dma_start(out=o, in_=i), inc="st", n=16)
        em.wait("sync", ("st", em.ctr["st"]))

        em.check_deadlock()

        with nc.Block() as block:
            @block.tensor
            def _(t):
                for f in em.ops["tensor"]:
                    f(t)

            @block.vector
            def _(v):
                for f in em.ops["vector"]:
                    f(v)

            @block.scalar
            def _(sx):
                for f in em.ops["scalar"]:
                    f(sx)

            @block.gpsimd
            def _(gp):
                for f in em.ops["gpsimd"]:
                    f(gp)

            @block.sync
            def _(sy):
                for f in em.ops["sync"]:
                    f(sy)

    return nc


_NC_CACHE = None


def kernel(L0_attn_x, L0_attn_Wq, L0_attn_bq, L0_attn_Wk, L0_attn_bk,
           L0_attn_Wv, L0_attn_bv, L0_attn_Wo, L0_attn_bo,
           L0_ffn_gate_weight, L0_ffn_e_bias, L0_ffn_gate_proj,
           L0_ffn_up_proj, L0_ffn_down_proj):
    global _NC_CACHE
    x = np.ascontiguousarray(np.asarray(L0_attn_x, dtype=np.float32))
    f32 = lambda a: np.ascontiguousarray(np.asarray(a, dtype=np.float32))

    wqt = f32(np.asarray(L0_attn_Wq).T)
    wkt = f32(np.asarray(L0_attn_Wk).T)
    wvt = f32(np.asarray(L0_attn_Wv).T)
    wot = f32(np.asarray(L0_attn_Wo).T)
    gwt = f32(np.asarray(L0_ffn_gate_weight).T)
    gT = f32(np.asarray(L0_ffn_gate_proj).transpose(0, 2, 1))
    uT = f32(np.asarray(L0_ffn_up_proj).transpose(0, 2, 1))
    dT = f32(np.asarray(L0_ffn_down_proj).transpose(0, 2, 1))

    bq = f32(L0_attn_bq)
    bk = f32(L0_attn_bk)
    bv = f32(L0_attn_bv)
    bo = f32(L0_attn_bo)
    bias_cm = np.concatenate([bq.reshape(8, 128).T, bk.reshape(8, 128).T],
                             axis=1)
    bias_cm = f32(bias_cm)
    brows = f32(np.concatenate([bk, bv, bo]))
    ebb = f32(np.broadcast_to(np.asarray(L0_ffn_e_bias, np.float32)[None, :], (128, E)))
    mask = f32(np.triu(np.ones((128, 128), np.float32)))
    ident = f32(np.eye(128, dtype=np.float32))

    if _NC_CACHE is None:
        _NC_CACHE = build_program()
    nc = _NC_CACHE

    shared = dict(wqt=wqt, wkt=wkt, wvt=wvt, wot=wot, gwt=gwt,
                  gateT=gT, upT=uT, downT=dT, bias_cm=bias_cm, brows=brows,
                  ebb=ebb, mask=mask, ident=ident)
    in_maps = [dict(shared, x=np.ascontiguousarray(x[i])) for i in range(8)]
    res = run_bass_kernel_spmd(nc, in_maps, core_ids=list(range(8)))
    out = np.stack([res.results[i]["y"] for i in range(8)], axis=0)
    return out.astype(np.float32)


# version 17
# speedup vs baseline: 1.0104x; 1.0104x over previous
import numpy as np
from contextlib import ExitStack

import concourse.bass as bass
import concourse.mybir as mybir
from concourse.bass_utils import run_bass_kernel_spmd

F32 = mybir.dt.float32
BF16 = mybir.dt.bfloat16
U8 = mybir.dt.uint8
AF = mybir.ActivationFunctionType
ALU = mybir.AluOpType

D = 1024
T = 2048
H = 16
NDC = 8
NT = 16
NG = 4
E = 16
EPS_LN = 1e-5
EPS_Z = 1e-6

ENGS = ("tensor", "vector", "scalar", "gpsimd", "sync")


class Em:
    def __init__(self, nc, sems):
        self.nc = nc
        self.sems = sems
        self.ops = {k: [] for k in ENGS}
        self.ctr = {k: 0 for k in sems}
        self.waited = {k: {} for k in ENGS}
        self.trace = {k: [] for k in ENGS}

    def wait(self, eng, dep):
        if dep is None:
            return
        if isinstance(dep, list):
            for d in dep:
                self.wait(eng, d)
            return
        sem, val = dep
        if val <= 0 or self.waited[eng].get(sem, 0) >= val:
            return
        self.waited[eng][sem] = val
        self.trace[eng].append(("w", sem, val))
        self.ops[eng].append(lambda e, s=self.sems[sem], v=val: e.wait_ge(s, v))

    def emit(self, eng, fn, inc=None, n=1):
        if inc is None:
            self.ops[eng].append(fn)
            return None
        self.ctr[inc] += n
        val = self.ctr[inc]
        self.trace[eng].append(("i", inc, n))
        self.ops[eng].append(
            lambda e, f=fn, s=self.sems[inc], k=n: f(e).then_inc(s, k))
        return (inc, val)

    def check_deadlock(self):
        pos = {k: 0 for k in ENGS}
        cnt = {k: 0 for k in self.ctr}
        progress = True
        while progress:
            progress = False
            for eng in ENGS:
                tr = self.trace[eng]
                while pos[eng] < len(tr):
                    kind, sem, v = tr[pos[eng]]
                    if kind == "w" and cnt[sem] < v:
                        break
                    if kind == "i":
                        cnt[sem] += v
                    pos[eng] += 1
                    progress = True
        stuck = {e: (pos[e], len(self.trace[e]), self.trace[e][pos[e]]
                     if pos[e] < len(self.trace[e]) else None) for e in ENGS
                 if pos[e] < len(self.trace[e])}
        if stuck:
            raise RuntimeError(f"deadlock: {stuck}; cnt={cnt}")

    def pe_now(self):
        return ("pe", self.ctr["pe"])


class Bank:
    def __init__(self, ap):
        self.ap = ap
        self.free = []


def build_program():
    nc = bass.Bass("TRN2", target_bir_lowering=False)

    x_d = nc.dram_tensor("x", [T, D], F32, kind="ExternalInput").ap()
    wqt_d = nc.dram_tensor("wqt", [D, D], F32, kind="ExternalInput").ap()
    wkt_d = nc.dram_tensor("wkt", [D, D], F32, kind="ExternalInput").ap()
    wvt_d = nc.dram_tensor("wvt", [D, D], F32, kind="ExternalInput").ap()
    wot_d = nc.dram_tensor("wot", [D, D], F32, kind="ExternalInput").ap()
    gwt_d = nc.dram_tensor("gwt", [D, E], F32, kind="ExternalInput").ap()
    gT_d = nc.dram_tensor("gateT", [E, D, 512], F32, kind="ExternalInput").ap()
    uT_d = nc.dram_tensor("upT", [E, D, 512], F32, kind="ExternalInput").ap()
    dT_d = nc.dram_tensor("downT", [E, 512, D], F32, kind="ExternalInput").ap()
    bias_cm_d = nc.dram_tensor("bias_cm", [128, 16], F32, kind="ExternalInput").ap()
    brows_d = nc.dram_tensor("brows", [3 * D], F32, kind="ExternalInput").ap()
    ebb_d = nc.dram_tensor("ebb", [128, E], F32, kind="ExternalInput").ap()
    mask_d = nc.dram_tensor("mask", [128, 128], F32, kind="ExternalInput").ap()
    ident_d = nc.dram_tensor("ident", [128, 128], F32, kind="ExternalInput").ap()
    y_d = nc.dram_tensor("y", [T, D], F32, kind="ExternalOutput").ap()

    x_tm = nc.alloc_sbuf_tensor("x_tm", [128, NT * D], F32).ap()

    arena_w = nc.alloc_sbuf_tensor("arena_w", [128, 65536], U8).ap()
    wqt_sb = arena_w[:, 0:16384].bitcast(BF16)
    wkt_sb = arena_w[:, 16384:32768].bitcast(BF16)
    wvt_sb = arena_w[:, 32768:49152].bitcast(BF16)
    wot_sb = arena_w[:, 49152:65536].bitcast(BF16)
    moe_slot = []
    for s in range(2):
        base = s * 24576
        moe_slot.append(dict(
            g=arena_w[:, base:base + 8192].bitcast(BF16),
            u=arena_w[:, base + 8192:base + 16384].bitcast(BF16),
            d=arena_w[:, base + 16384:base + 24576].bitcast(BF16)))
    rbase = 49152

    def rview(off, cols):
        return arena_w[:, rbase + off * 4: rbase + (off + cols) * 4].bitcast(F32)
    escore = rview(0, 256)
    sfc = rview(256, 256)
    tmpsc = rview(512, 256)
    rank_e = rview(768, 256)
    combine = rview(1024, 256)
    gs = rview(1280, 64)
    rank_g = rview(1344, 64)
    gmask = rview(1408, 64)
    ptree = rview(1472, 640)
    cmpb = [rview(2112 + i * 256, 256) for i in range(3)]
    dtree = rview(2880, 224)
    den = rview(3104, 16)
    rden = rview(3120, 16)
    spc = rview(3136, 16)

    arena_h = nc.alloc_sbuf_tensor("arena_h", [128, 32768], U8).ap()
    ln_tm_slot = [arena_h[:, 0:2048].bitcast(BF16),
                  arena_h[:, 2048:4096].bitcast(BF16),
                  arena_h[:, 4096:6144].bitcast(BF16),
                  arena_h[:, 6144:8192].bitcast(BF16)]
    attn_tm_slot = [arena_h[:, 4096:6144].bitcast(BF16),
                    arena_h[:, 6144:8192].bitcast(BF16)]
    h1_fm = arena_h[:, 8192:16384].bitcast(BF16)
    attn_fm = arena_h[:, 16384:24576].bitcast(BF16)
    h2_fm = arena_h[:, 0:32768].bitcast(BF16)
    brows_sb = arena_h[:, 24576:30720].bitcast(BF16)
    A_m = arena_h[:, 30720:31744].bitcast(BF16)
    mask_sb = arena_h[:, 31744:32256].bitcast(F32)
    ones_row = arena_h[:, 32256:32512].bitcast(BF16)
    bias_cm = arena_h[:, 32512:32576].bitcast(F32)
    zr_sb = arena_h[:, 32576:32640].bitcast(F32)

    arena_p = nc.alloc_sbuf_tensor("arena_p", [128, 33024], U8).ap()
    q_grp = arena_p[:, 0:8192].bitcast(BF16)
    k_grp = arena_p[:, 8192:16384].bitcast(BF16)
    ktm_grp = arena_p[:, 16384:24576].bitcast(BF16)
    vaug_grp = arena_p[:, 24576:33024].bitcast(BF16)
    act_slot = [arena_p[:, 0:16384].bitcast(BF16),
                arena_p[:, 16384:32768].bitcast(BF16)]
    ln1_junk = arena_p[:, 0:4096].bitcast(F32)
    ln2_junk = arena_p[:, 16384:20480].bitcast(F32)

    scr = nc.alloc_sbuf_tensor("scr", [128, 8192], U8).ap()
    m_scr = [scr[:, 0:2048].bitcast(F32), scr[:, 4096:6144].bitcast(F32)]
    e_scr = [scr[:, 2048:4096].bitcast(F32), scr[:, 6144:8192].bitcast(F32)]
    mask4 = scr[:, 6144:7168].bitcast(BF16)
    A_m2 = scr[:, 7168:8192].bitcast(BF16)
    sg_scr = [scr[:, 0:2048].bitcast(F32), scr[:, 2048:4096].bitcast(F32)]
    h2_tm_slot0 = scr[:, 4096:6144].bitcast(BF16)

    xtra = nc.alloc_sbuf_tensor("xtra", [128, 2048], U8).ap()
    A_m_w1 = xtra[:, 0:1024].bitcast(BF16)
    A_m2_w1 = xtra[:, 1024:2048].bitcast(BF16)
    h2_tm_slot1 = xtra[:, 0:2048].bitcast(BF16)
    h2_tm_slot2 = scr[:, 6144:8192].bitcast(BF16)
    h2_tm_slot = [h2_tm_slot0, h2_tm_slot1, h2_tm_slot2]

    spc_a = nc.alloc_sbuf_tensor("spc_a", [128, 16], F32).ap()
    S_f32 = nc.alloc_sbuf_tensor("S_f32", [128, 528], F32).ap()
    S_bf = nc.alloc_sbuf_tensor("S_bf", [128, 528], BF16).ap()
    ident_sb = nc.alloc_sbuf_tensor("ident_sb", [128, 128], BF16).ap()
    ebb_sb = nc.alloc_sbuf_tensor("ebb_sb", [128, E], F32).ap()
    gwt_sb = nc.alloc_sbuf_tensor("gwt_sb", [128, NDC * E], BF16).ap()
    stats = nc.alloc_sbuf_tensor("stats", [128, 120], F32).ap()

    fb = [Bank(nc.alloc_psum_tensor(f"pb{i}", [128, 512], F32).ap()) for i in range(6)]
    tb = [Bank(nc.alloc_psum_tensor(f"tbk{i}", [128, 1024], BF16).ap()) for i in range(2)]

    with ExitStack() as stack:
        sems = {}
        for name in ("lx0", "lx1", "lx2", "lx3", "lx4", "lx5", "ldc",
                     "lwa", "lwq", "lwk", "lwv", "lwo", "lw0", "lw1",
                     "pe", "dve", "act", "gp", "st"):
            sems[name] = stack.enter_context(nc.semaphore(name))
        em = Em(nc, sems)

        d_x_dep = [None] * NT
        xv = x_d.rearrange("(tt p) d -> p tt d", p=128)

        xsems = ["lx0", "lx1", "lx2", "lx3", "lx4", "lx5"]
        xsem_i = [0]

        def load_x(t0, ntl):
            sem = xsems[xsem_i[0]]
            xsem_i[0] += 1
            ev = em.emit("sync", lambda e: e.dma_start(
                out=x_tm[:, t0 * D:(t0 + ntl) * D].rearrange("p (tt d) -> p tt d", d=D),
                in_=xv[:, t0:t0 + ntl, :]), inc=sem, n=16)
            for t in range(t0, t0 + ntl):
                d_x_dep[t] = ev
        em.emit("sync", lambda e: e.dma_start(out=mask_sb[:, :], in_=mask_d), inc="ldc", n=16)
        em.emit("sync", lambda e: e.dma_start(out=bias_cm[:, :], in_=bias_cm_d), inc="ldc", n=16)
        load_x(0, 1)
        load_x(1, 1)
        load_x(2, 2)
        for g in range(1, NG):
            load_x(4 * g, 4)
        em.emit("sync", lambda e: e.dma_start(out=ebb_sb[:, :], in_=ebb_d), inc="ldc", n=16)
        d_mask = d_bcm = d_ebb = ("ldc", 48)

        def wload(sem, out_sb, in_ap):
            return em.emit("gpsimd", lambda e, o=out_sb, i=in_ap: e.dma_start(out=o, in_=i),
                           inc=sem, n=16)

        wload("lwa", ident_sb[:, :], ident_d)
        wload("lwa", brows_sb[0:1, :], brows_d[None, :])
        d_ident = d_brows = ("lwa", 32)
        d_wq = wload("lwq", wqt_sb.rearrange("p (c o) -> p c o", c=NDC),
                     wqt_d.rearrange("(c p) o -> p c o", p=128))
        d_wk = wload("lwk", wkt_sb.rearrange("p (c o) -> p c o", c=NDC),
                     wkt_d.rearrange("(c p) o -> p c o", p=128))
        d_wv = wload("lwv", wvt_sb.rearrange("p (c o) -> p c o", c=NDC),
                     wvt_d.rearrange("(c p) o -> p c o", p=128))
        wload("lwo", wot_sb.rearrange("p (c o) -> p c o", c=NDC),
              wot_d.rearrange("(c p) o -> p c o", p=128))
        wload("lwo", gwt_sb.rearrange("p (c o) -> p c o", c=NDC),
              gwt_d.rearrange("(c p) o -> p c o", p=128))
        d_wo = d_gwt = ("lwo", 32)

        em.emit("vector", lambda e: e.memset(ones_row[0:1, :], 1.0))
        em.emit("vector", lambda e: e.memset(S_f32[:, :], 0.0))
        em.emit("vector", lambda e: e.memset(S_bf[:, :], 0.0))
        em.emit("vector", lambda e: e.memset(
            vaug_grp.rearrange("p (g c) -> p g c", c=66)[:, :, 64:65], 1.0))
        d_const = em.emit("vector", lambda e: e.memset(
            vaug_grp.rearrange("p (g c) -> p g c", c=66)[:, :, 65:66], 0.0), inc="dve")

        bank_i = [0]

        def acquire_fb():
            b = fb[bank_i[0] % 6]
            bank_i[0] += 1
            em.wait("tensor", b.free)
            b.free = []
            return b

        junk_last = {}

        def ln_stats(xt, st, junk, x_dep, junk_dep, jkey):
            em.wait("scalar", x_dep)
            em.wait("scalar", junk_dep)
            em.wait("scalar", junk_last.get(jkey))
            ev_c = em.emit("scalar", lambda e, o=junk, i=xt, a=st[:, 0:1]:
                           e.activation(o, i, AF.Copy, accum_out=a), inc="act")
            em.wait("scalar", ev_c)
            ev = em.emit("scalar", lambda e, o=junk, i=xt, a=st[:, 1:2]:
                         e.activation(o, i, AF.Square, accum_out=a), inc="act")
            junk_last[jkey] = ev
            return ev

        smalls_last = {}
        sqrt_last = {}

        def ln_smalls(st, ev_stats, skey):
            em.wait("vector", ev_stats)
            em.wait("vector", smalls_last.get(skey))
            em.emit("vector", lambda e, o=st[:, 2:3], i=st[:, 0:1]:
                    e.tensor_scalar_mul(o, i, 1.0 / D))
            em.emit("vector", lambda e, o=st[:, 3:4], i=st[:, 1:2]:
                    e.tensor_scalar(o, i, 1.0 / D, EPS_LN, op0=ALU.mult, op1=ALU.add))
            em.emit("vector", lambda e, o=st[:, 4:5], i=st[:, 2:3]: e.tensor_mul(o, i, i))
            em.emit("vector", lambda e, o=st[:, 9:10], i=st[:, 0:1]:
                    e.tensor_scalar_mul(o, i, 1.0))
            em.emit("vector", lambda e, o=st[:, 5:6], i=st[:, 3:4], m2=st[:, 4:5]:
                    e.tensor_sub(o, i, m2))
            em.emit("vector", lambda e, o=st[:, 9:10], i=st[:, 0:1]:
                    e.tensor_scalar_mul(o, i, 1.0))
            ev = em.emit("vector", lambda e, o=st[:, 6:7], i=st[:, 5:6]:
                         e.reciprocal(o, i), inc="dve")
            smalls_last[skey] = ev
            return ev

        def ln_sqrt(st, ev_v, skey):
            em.wait("scalar", ev_v)
            em.wait("scalar", sqrt_last.get(skey))
            ev = em.emit("scalar", lambda e, o=st[:, 7:8], i=st[:, 6:7]:
                         e.activation(o, i, AF.Sqrt), inc="act")
            sqrt_last[skey] = ev
            return ev

        def ln_fin(xt, out_ap, st, ev_r, deps_out):
            em.wait("vector", ev_r)
            em.wait("vector", deps_out)
            return em.emit("vector", lambda e, o=out_ap, i=xt, m=st[:, 2:3], r=st[:, 7:8]:
                           e.tensor_scalar(o, i, m, r, op0=ALU.subtract, op1=ALU.mult),
                           inc="dve")

        exp_loads = [None] * E
        slot_dead = [None, None]

        def emit_exp_load(e_):
            s = e_ % 2
            sem = "lw0" if s == 0 else "lw1"
            em.wait("gpsimd", slot_dead[s])
            wload(sem, moe_slot[s]["g"].rearrange("p (c i) -> p c i", c=NDC),
                  gT_d[e_].rearrange("(c p) i -> p c i", p=128))
            wload(sem, moe_slot[s]["u"].rearrange("p (c i) -> p c i", c=NDC),
                  uT_d[e_].rearrange("(c p) i -> p c i", p=128))
            exp_loads[e_] = wload(sem, moe_slot[s]["d"].rearrange("p (c o) -> p c o", c=4),
                                  dT_d[e_].rearrange("(c p) o -> p c o", p=128))

        qk_free = []
        ktm_free = []
        vaug_free = []
        attn_tm_free = [[], []]
        lnslot_free = [[], [], [], []]
        zr_last = [None]
        scr_free = [[], []]
        am_free = [[], []]
        mirror_ev = [None, None]
        x_tile_y1 = [None] * NT
        ln_s_ev = [None] * NT
        gp_a3_final = None

        def tm_tile_to_fm(tm_ap, fm_ap, tl, bslot, ready_dep, evac_eng="vector",
                          bank=None):
            b = bank if bank is not None else tb[bslot]
            bap = b.ap if bank is None else b.ap.bitcast(BF16)
            em.wait("tensor", b.free)
            b.free = []
            em.wait("tensor", d_ident)
            last = None
            for dc in range(NDC):
                em.wait("tensor", ready_dep)
                last = em.emit("tensor", lambda e, o=bap[:, dc * 128:(dc + 1) * 128],
                               i=tm_ap[:, dc * 128:(dc + 1) * 128]:
                               e.transpose(o, i, ident_sb[:, :]),
                               inc="pe" if dc == NDC - 1 else None)
            oap = fm_ap.rearrange("p (c t) -> p c t", c=NDC)[:, :, tl * 128:(tl + 1) * 128]
            iap = bap[:, :].rearrange("p (c w) -> p c w", c=NDC)
            if evac_eng == "vector":
                em.wait("vector", last)
                ev = em.emit("vector", lambda e, o=oap, i=iap:
                             e.tensor_copy(o, i), inc="dve")
            else:
                em.wait("scalar", last)
                ev = em.emit("scalar", lambda e, o=oap, i=iap:
                             e.copy(o, i), inc="act")
            b.free = [ev]
            return last, ev

        ln1_evr = [None] * 4

        def ln1_vs_tile(g, tl):
            tt = 4 * g + tl
            st = stats[:, (tt % 4) * 10:(tt % 4) * 10 + 10]
            xt = x_tm[:, tt * D:(tt + 1) * D]
            ev_st = ln_stats(xt, st, ln1_junk, d_x_dep[tt], qk_free, 'j1')
            ev_v = ln_smalls(st, ev_st, ('a', tt % 4))
            ln1_evr[tl] = ln_sqrt(st, ev_v, ('a', tt % 4))

        def ln1_fin_tile(g, tl):
            tt = 4 * g + tl
            st = stats[:, (tt % 4) * 10:(tt % 4) * 10 + 10]
            xt = x_tm[:, tt * D:(tt + 1) * D]
            guards = lnslot_free[tl] + (attn_tm_free[tl - 2] if tl >= 2 else [])
            ln_s_ev[tt] = ln_fin(xt, ln_tm_slot[tl][:, :], st, ln1_evr[tl], guards)

        def ln1_T_tile(g, tl):
            tt = 4 * g + tl
            pe_last, evac = tm_tile_to_fm(ln_tm_slot[tl], h1_fm, tl, tl % 2,
                                          ln_s_ev[tt])
            lnslot_free[tl] = [pe_last]
            return evac

        ev_s2 = [None] * NT
        h2_free = [[], [], []]
        h2_ready = [None] * NT
        sig_evs = []
        ln2_evr = [None] * NT

        def lnv2_a(tt):
            xt = x_tm[:, tt * D:(tt + 1) * D]
            st = stats[:, 40 + (tt % 8) * 10: 40 + (tt % 8) * 10 + 10]
            ev_st = ln_stats(xt, st, ln2_junk, x_tile_y1[tt], None, 'j2')
            ev_v = ln_smalls(st, ev_st, ('b', tt % 8))
            ln2_evr[tt] = ln_sqrt(st, ev_v, ('b', tt % 8))

        def lnv2_b(tt):
            xt = x_tm[:, tt * D:(tt + 1) * D]
            st = stats[:, 40 + (tt % 8) * 10: 40 + (tt % 8) * 10 + 10]
            extra = [gp_a3_final] if tt % 3 != 1 else []
            ev_s2[tt] = ln_fin(xt, h2_tm_slot[tt % 3][:, :], st, ln2_evr[tt],
                               h2_free[tt % 3] + extra)

        for tl in range(4):
            ln1_vs_tile(0, tl)
            if tl >= 1:
                ln1_fin_tile(0, tl - 1)
        ln1_fin_tile(0, 3)
        h1fm_ready = [ln1_T_tile(0, tl) for tl in range(4)]

        for g in range(NG):
            def phi_evac(bank, last_pe, bias_ap, out_ap, si, out_free):
                em.wait("vector", last_pe)
                em.wait("vector", scr_free[si])
                if bias_ap is not None:
                    ev_m = em.emit("vector", lambda e, o=m_scr[si], i=bank.ap[:, :], ba=bias_ap:
                                   e.tensor_scalar(o, i, ba, 0.0, op0=ALU.add, op1=ALU.min),
                                   inc="dve")
                else:
                    ev_m = em.emit("vector", lambda e, o=m_scr[si], i=bank.ap[:, :]:
                                   e.tensor_scalar_min(o, i, 0.0), inc="dve")
                em.wait("scalar", ev_m)
                ev_e = em.emit("scalar", lambda e, o=e_scr[si], i=m_scr[si]:
                               e.activation(o, i, AF.Exp), inc="act")
                em.wait("scalar", last_pe)
                if bias_ap is not None:
                    em.wait("scalar", d_bcm)
                    ev_r = em.emit("scalar", lambda e, o=m_scr[si], i=bank.ap[:, :], ba=bias_ap:
                                   e.activation(o, i, AF.Relu, bias=ba), inc="act")
                else:
                    ev_r = em.emit("scalar", lambda e, o=m_scr[si], i=bank.ap[:, :]:
                                   e.activation(o, i, AF.Relu), inc="act")
                bank.free = [ev_r]
                em.wait("gpsimd", ev_e)
                em.wait("gpsimd", ev_r)
                em.wait("gpsimd", out_free)
                ev = em.emit("gpsimd", lambda e, o=out_ap, a=e_scr[si], bb=m_scr[si]:
                             e.tensor_add(o, a, bb), inc="gp")
                scr_free[si] = [ev]
                return ev

            q_ready, k_ready = [], []
            for which, w_sb, w_dep, bcol, out_grp, rdy in (
                    ("q", wqt_sb, d_wq, 0, q_grp, q_ready),
                    ("k", wkt_sb, d_wk, 8, k_grp, k_ready)):
                for oc in range(NDC):
                    b = acquire_fb()
                    em.wait("tensor", w_dep)
                    em.wait("tensor", h1fm_ready)
                    last = None
                    for dc in range(NDC):
                        last = em.emit("tensor", lambda e, o=b.ap[:, :],
                                       l=w_sb[:, dc * 1024 + oc * 128: dc * 1024 + oc * 128 + 128],
                                       r=h1_fm[:, dc * 512:(dc + 1) * 512],
                                       s=(dc == 0), p=(dc == NDC - 1):
                                       e.matmul(o, l, r, start=s, stop=p),
                                       inc="pe" if dc == NDC - 1 else None)
                    em.wait("vector", d_bcm)
                    bap = bias_cm[:, bcol + oc: bcol + oc + 1]
                    ev = phi_evac(b, last, bap, out_grp[:, oc * 512:(oc + 1) * 512],
                                  oc % 2, qk_free)
                    rdy.append(ev)
            qk_free = []
            if g == NG - 1:
                slot_dead[0] = em.pe_now()
                emit_exp_load(0)

            v_ready = []
            for tl in range(4):
                evs_tl = []
                for dh in range(2):
                    b = acquire_fb()
                    em.wait("tensor", [d_wv, d_brows])
                    em.wait("tensor", h1fm_ready)
                    for dc in range(NDC):
                        em.emit("tensor", lambda e, o=b.ap[:, :],
                                l=h1_fm[:, dc * 512 + tl * 128: dc * 512 + tl * 128 + 128],
                                r=wvt_sb[:, dc * 1024 + dh * 512: dc * 1024 + dh * 512 + 512],
                                s=(dc == 0): e.matmul(o, l, r, start=s, stop=False))
                    last = em.emit("tensor", lambda e, o=b.ap[:, :],
                                   r=brows_sb[0:1, D + dh * 512: D + dh * 512 + 512]:
                                   e.matmul(o, ones_row[0:1, :], r, start=False, stop=True),
                                   inc="pe")
                    em.wait("vector", last)
                    em.wait("vector", vaug_free)
                    ev = em.emit("vector", lambda e,
                                 o=vaug_grp[:, tl * 1056 + 528 * dh: tl * 1056 + 528 * dh + 528]
                                 .rearrange("p (h c) -> p h c", c=66)[:, :, 0:64],
                                 i=b.ap[:, :].rearrange("p (h c) -> p h c", c=64):
                                 e.tensor_copy(o, i), inc="dve")
                    b.free = [ev]
                    evs_tl.append(ev)
                v_ready.append(evs_tl)
            vaug_free = []
            if g == NG - 1:
                slot_dead[1] = em.pe_now()
                emit_exp_load(1)
                gp_a3_final = ("gp", em.ctr["gp"])

            ktm_ready = []
            for tl in range(4):
                b = tb[tl % 2]
                em.wait("tensor", b.free)
                b.free = []
                em.wait("tensor", d_ident)
                em.wait("tensor", k_ready)
                last = None
                for oc in range(NDC):
                    last = em.emit("tensor", lambda e, o=b.ap[:, oc * 128:(oc + 1) * 128],
                                   i=k_grp[:, oc * 512 + tl * 128: oc * 512 + tl * 128 + 128]:
                                   e.transpose(o, i, ident_sb[:, :]),
                                   inc="pe" if oc == NDC - 1 else None)
                em.wait("vector", last)
                em.wait("vector", ktm_free)
                ev = em.emit("vector", lambda e,
                             o=ktm_grp[:, tl * 1024:(tl + 1) * 1024],
                             i=b.ap[:, :]: e.tensor_copy(o, i), inc="dve")
                b.free = [ev]
                ktm_ready.append([ev])
            ktm_free = []

            em.wait("vector", q_ready + k_ready)
            em.wait("vector", d_mask)
            for c4 in range(4):
                em.emit("vector", lambda e, o=mask4[:, c4 * 128:(c4 + 1) * 128],
                        i=mask_sb[:, :]: e.tensor_copy(o, i))

            bA = [fb[0], fb[1]]
            bO0 = [fb[2], fb[3]]
            bS = [fb[4], fb[5]]
            am_bufs = [(A_m, A_m2), (A_m_w1, A_m2_w1)]
            at_ready = [None] * 4
            attn_done_last = [None] * 4

            def emit_A6(cl):
                pe_last, evac = tm_tile_to_fm(attn_tm_slot[cl % 2], attn_fm, cl,
                                              None, attn_done_last[cl],
                                              bank=bS[cl % 2])
                attn_tm_free[cl % 2] = [pe_last]
                at_ready[cl] = evac

            for cl in range(4):
                if cl > 0:
                    emit_A6(cl - 1)
                wbanks = [(bA[0], bA[1]), (tb[0], tb[1])]
                waps = []
                ev_a = []
                for w in range(2):
                    b0, b1 = wbanks[w]
                    a0 = b0.ap if w == 0 else b0.ap.bitcast(F32)
                    a1 = b1.ap if w == 0 else b1.ap.bitcast(F32)
                    waps.append((a0, a1))
                    em.wait("tensor", b0.free + b1.free)
                    b0.free = []
                    b1.free = []
                    eve = evo = None
                    for jj in range(4):
                        j = 4 * w + jj
                        win = j * 512 + cl * 128
                        em.wait("tensor", [q_ready[j], k_ready[j]])
                        eve = em.emit("tensor", lambda e, o=a0[:, jj * 128:jj * 128 + 128],
                                      l=k_grp[0:64, win:win + 128],
                                      r=q_grp[0:64, win:win + 128],
                                      s=(jj == 0), p=(jj == 3):
                                      e.matmul(o, l, r, start=s, stop=p),
                                      inc="pe" if jj == 3 else None)
                        evo = em.emit("tensor", lambda e, o=a1[:, jj * 128:jj * 128 + 128],
                                      l=k_grp[64:128, win:win + 128],
                                      r=q_grp[64:128, win:win + 128],
                                      s=(jj == 0), p=(jj == 3):
                                      e.matmul(o, l, r, start=s, stop=p,
                                               tile_position=(64, 0)),
                                      inc="pe" if jj == 3 else None)
                    ev_a.append([eve, evo])

                ev_me = [None, None]
                ev_mo = [None, None]
                for w in range(2):
                    a0, a1 = waps[w]
                    ame, amo = am_bufs[w]
                    em.wait("vector", ev_a[w])
                    em.wait("vector", am_free[w])
                    ev_me[w] = em.emit("vector", lambda e, i=a0, o=ame:
                                       e.tensor_mul(o[:, :], i, mask4[:, :]), inc="dve")
                    ev_mo[w] = em.emit("vector", lambda e, i=a1, o=amo:
                                       e.tensor_mul(o[:, :], i, mask4[:, :]), inc="dve")
                    if w == 0:
                        bA[0].free = [ev_me[0]]
                        bA[1].free = [ev_mo[0]]

                for w in range(2):
                    ame, amo = am_bufs[w]
                    av_done = None
                    if w == 0:
                        bo0, bo1 = bO0[0], bO0[1]
                        o0, o1 = bo0.ap, bo1.ap
                    else:
                        bo0, bo1 = tb[0], tb[1]
                        o0, o1 = bo0.ap.bitcast(F32), bo1.ap.bitcast(F32)

                    def vap_of(h):
                        return vaug_grp[:, cl * 1056 + 66 * h: cl * 1056 + 66 * h + 66]

                    if w == 0:
                        em.wait("tensor", bo0.free + bo1.free)
                        bo0.free = []
                        bo1.free = []
                    em.wait("tensor", ev_me[w])
                    em.wait("tensor", v_ready[cl] + ktm_ready[cl])
                    for jj in range(4):
                        em.emit("tensor", lambda e, o=o0[:, 66 * jj:66 * jj + 66],
                                l=ame[:, jj * 128:jj * 128 + 128],
                                r=vap_of(2 * (4 * w + jj)), s=(jj == 0):
                                e.matmul(o, l, r, start=s, stop=False))
                    em.wait("tensor", ev_mo[w])
                    for jj in range(4):
                        av_done = em.emit("tensor", lambda e, o=o1[:, 66 * jj:66 * jj + 66],
                                l=amo[:, jj * 128:jj * 128 + 128],
                                r=vap_of(2 * (4 * w + jj) + 1), s=(jj == 0):
                                e.matmul(o, l, r, start=s, stop=False),
                                inc="pe" if jj == 3 else None)
                    am_free[w] = [av_done]
                    em.wait("tensor", mirror_ev[w] if mirror_ev[w] else d_const)
                    for jj in range(4):
                        j = 4 * w + jj
                        win = j * 512 + cl * 128
                        em.emit("tensor", lambda e, o=o0[:, 66 * jj:66 * jj + 66],
                                l=q_grp[0:64, win:win + 128],
                                r=S_bf[0:64, 66 * j:66 * j + 66], p=(jj == 3):
                                e.matmul(o, l, r, start=False, stop=p))
                    ev_qs = None
                    for jj in range(4):
                        j = 4 * w + jj
                        win = j * 512 + cl * 128
                        ev_qs = em.emit("tensor", lambda e, o=o1[:, 66 * jj:66 * jj + 66],
                                l=q_grp[64:128, win:win + 128],
                                r=S_bf[64:128, 66 * j:66 * j + 66], p=(jj == 3):
                                e.matmul(o, l, r, start=False, stop=p,
                                         tile_position=(64, 0)),
                                inc="pe" if jj == 3 else None)
                    em.wait("tensor", bS[0].free + bS[1].free)
                    bS[0].free = []
                    bS[1].free = []
                    pe_wave_last = None
                    for jj in range(4):
                        em.emit("tensor", lambda e, o=bS[0].ap[0:64, 66 * jj:66 * jj + 66],
                                l=ktm_grp[:, cl * 1024 + 128 * (4 * w + jj):
                                          cl * 1024 + 128 * (4 * w + jj) + 64],
                                r=vap_of(2 * (4 * w + jj)), s=(jj == 0), p=(jj == 3):
                                e.matmul(o, l, r, start=s, stop=p))
                        pe_wave_last = em.emit("tensor", lambda e,
                                o=bS[1].ap[0:64, 66 * jj:66 * jj + 66],
                                l=ktm_grp[:, cl * 1024 + 128 * (4 * w + jj) + 64:
                                          cl * 1024 + 128 * (4 * w + jj) + 128],
                                r=vap_of(2 * (4 * w + jj) + 1), s=(jj == 0), p=(jj == 3):
                                e.matmul(o, l, r, start=s, stop=p), inc="pe")
                    em.wait("vector", ev_qs)
                    for par, oo in ((0, o0), (1, o1)):
                        zin = oo[:, 0:264].rearrange("p (j c) -> p j c", c=66)[:, :, 64:65]
                        em.emit("vector", lambda e,
                                o=zr_sb[:, 8 * par:8 * par + 4].rearrange("p (j o) -> p j o", o=1),
                                i=zin: e.tensor_scalar_add(o, i, EPS_Z))
                    for par in range(2):
                        em.emit("vector", lambda e, o=zr_sb[:, 8 * par + 4:8 * par + 8],
                                i=zr_sb[:, 8 * par:8 * par + 4]: e.reciprocal(o, i))
                    em.wait("vector", zr_last[0])
                    zr_ev = em.emit("vector", lambda e: e.tensor_scalar_mul(
                        spc_a[:, 0:8], zr_sb[:, 0:8], 1.0), inc="dve")
                    zr_last[0] = zr_ev
                    em.wait("scalar", zr_ev)
                    em.wait("scalar", attn_tm_free[cl % 2])
                    evs0 = []
                    for jj in range(4):
                        h = 8 * w + 2 * jj
                        evv = em.emit("scalar", lambda e,
                                      o=attn_tm_slot[cl % 2][:, 64 * h: 64 * h + 64],
                                      i=o0[:, 66 * jj:66 * jj + 64],
                                      r=zr_sb[:, 4 + jj:5 + jj]:
                                      e.activation(o, i, AF.Copy, scale=r), inc="act")
                        evs0.append(evv)
                    bo0.free = evs0
                    em.wait("vector", attn_tm_free[cl % 2])
                    evs1 = []
                    for jj in range(4):
                        h = 8 * w + 2 * jj + 1
                        evv = em.emit("vector", lambda e,
                                      o=attn_tm_slot[cl % 2][:, 64 * h: 64 * h + 64],
                                      i=o1[:, 66 * jj:66 * jj + 64],
                                      r=zr_sb[:, 12 + jj:13 + jj]:
                                      e.tensor_scalar_mul(o, i, r), inc="dve")
                        evs1.append(evv)
                    bo1.free = evs1
                    attn_done_last[cl] = [evs0[-1], evs1[-1]]
                    scol = 264 * w
                    em.wait("vector", pe_wave_last)
                    em.emit("vector", lambda e, o=S_f32[0:64, scol:scol + 264],
                            i=bS[0].ap[0:64, 0:264]: e.tensor_add(o, o, i))
                    ev_su = em.emit("vector", lambda e, o=S_f32[64:128, scol:scol + 264],
                                    i=bS[1].ap[0:64, 0:264]: e.tensor_add(o, o, i), inc="dve")
                    bS[0].free = [ev_su]
                    bS[1].free = [ev_su]
                    em.wait("scalar", ev_su)
                    mirror_ev[w] = em.emit("scalar", lambda e, o=S_bf[:, scol:scol + 264],
                                           i=S_f32[:, scol:scol + 264]: e.copy(o, i), inc="act")
            qk_free = [em.pe_now()]
            scr_free[1] = scr_free[1] + [em.pe_now()]
            ktm_free = [em.pe_now()]
            vaug_free = [em.pe_now()]

            nxt = g + 1 < NG
            nready = []
            for tl in range(4):
                if tl == 2:
                    emit_A6(3)
                tt = 4 * g + tl
                for dh in range(2):
                    b = acquire_fb()
                    em.wait("tensor", [d_wo, d_brows])
                    em.wait("tensor", at_ready[tl])
                    for dc in range(NDC):
                        em.emit("tensor", lambda e, o=b.ap[:, :],
                                l=attn_fm[:, dc * 512 + tl * 128: dc * 512 + tl * 128 + 128],
                                r=wot_sb[:, dc * 1024 + dh * 512: dc * 1024 + dh * 512 + 512],
                                s=(dc == 0): e.matmul(o, l, r, start=s, stop=False))
                    last = em.emit("tensor", lambda e, o=b.ap[:, :],
                                   r=brows_sb[0:1, 2 * D + dh * 512: 2 * D + dh * 512 + 512]:
                                   e.matmul(o, ones_row[0:1, :], r, start=False, stop=True),
                                   inc="pe")
                    em.wait("vector", last)
                    ev = em.emit("vector", lambda e,
                                 o=x_tm[:, tt * D + dh * 512: tt * D + dh * 512 + 512],
                                 i=b.ap[:, :]:
                                 e.scalar_tensor_tensor(o, i, 1.0, o, op0=ALU.mult, op1=ALU.add),
                                 inc="dve")
                    b.free = [ev]
                    x_tile_y1[tt] = ev
                if nxt:
                    ln1_vs_tile(g + 1, tl)
                    if tl >= 1:
                        ln1_fin_tile(g + 1, tl - 1)
                    if tl >= 2:
                        nready.append(ln1_T_tile(g + 1, tl - 2))
                else:
                    if tl <= 2:
                        lnv2_a(tl)
                    if tl >= 2:
                        lnv2_b(tl - 2)
            if nxt:
                ln1_fin_tile(g + 1, 3)
                nready.append(ln1_T_tile(g + 1, 2))
                nready.append(ln1_T_tile(g + 1, 3))
                h1fm_ready = nready

        act_free = [[], []]
        sg_free = [[], []]
        act_evs_by_e = [dict() for _ in range(E)]
        x_final = [None] * NT

        def emit_gu(e_, tc):
            s = e_ % 2
            aslot = act_slot[s]
            for it in range(4):
                bg = acquire_fb()
                em.wait("tensor", exp_loads[e_])
                em.wait("tensor", [h2_ready[4 * tc + i] for i in range(4)])
                ev_g = None
                for dc in range(NDC):
                    ev_g = em.emit("tensor", lambda e, o=bg.ap[:, :],
                                   l=moe_slot[s]["g"][:, dc * 512 + it * 128: dc * 512 + it * 128 + 128],
                                   r=h2_fm[:, dc * 2048 + tc * 512: dc * 2048 + tc * 512 + 512],
                                   st=(dc == 0), sp=(dc == NDC - 1):
                                   e.matmul(o, l, r, start=st, stop=sp),
                                   inc="pe" if dc == NDC - 1 else None)
                bu = acquire_fb()
                ev_u = None
                for dc in range(NDC):
                    ev_u = em.emit("tensor", lambda e, o=bu.ap[:, :],
                                   l=moe_slot[s]["u"][:, dc * 512 + it * 128: dc * 512 + it * 128 + 128],
                                   r=h2_fm[:, dc * 2048 + tc * 512: dc * 2048 + tc * 512 + 512],
                                   st=(dc == 0), sp=(dc == NDC - 1):
                                   e.matmul(o, l, r, start=st, stop=sp),
                                   inc="pe" if dc == NDC - 1 else None)
                si = (tc * 4 + it) % 2
                em.wait("scalar", ev_g)
                em.wait("scalar", sg_free[si])
                em.wait("scalar", gp_a3_final)
                ev_sg = em.emit("scalar", lambda e, o=sg_scr[si], i=bg.ap[:, :]:
                                e.activation(o, i, AF.Silu), inc="act")
                bg.free = [ev_sg]
                em.wait("vector", [ev_u, ev_sg])
                em.wait("vector", act_free[s])
                ev_a = em.emit("vector", lambda e,
                               o=aslot[:, it * 2048 + tc * 512: it * 2048 + tc * 512 + 512],
                               i=bu.ap[:, :], sg=sg_scr[si]:
                               e.scalar_tensor_tensor(o, i, 1.0, sg, op0=ALU.mult, op1=ALU.mult),
                               inc="dve")
                bu.free = [ev_a]
                sg_free[si] = [ev_a]
                act_evs_by_e[e_][(tc, it)] = ev_a

        def emit_down(e_, comb_ev):
            s = e_ % 2
            aslot = act_slot[s]
            for tt in range(NT):
                tc = tt // 4
                for dh in range(2):
                    bd = acquire_fb()
                    em.wait("tensor", [act_evs_by_e[e_][(tc, it)] for it in range(4)])
                    last = None
                    for it in range(4):
                        last = em.emit("tensor", lambda e, o=bd.ap[:, :],
                                       l=aslot[:, it * 2048 + tt * 128: it * 2048 + tt * 128 + 128],
                                       r=moe_slot[s]["d"][:, it * 1024 + dh * 512: it * 1024 + dh * 512 + 512],
                                       st=(it == 0), sp=(it == 3):
                                       e.matmul(o, l, r, start=st, stop=sp),
                                       inc="pe" if it == 3 else None)
                    em.wait("vector", last)
                    em.wait("vector", comb_ev)
                    ev = em.emit("vector", lambda e,
                                 o=x_tm[:, tt * D + dh * 512: tt * D + dh * 512 + 512],
                                 i=bd.ap[:, :], c=combine[:, e_ * 16 + tt: e_ * 16 + tt + 1]:
                                 e.scalar_tensor_tensor(o, i, c, o, op0=ALU.mult, op1=ALU.add),
                                 inc="dve")
                    bd.free = [ev]
                    if e_ == E - 1:
                        x_final[tt] = ev
            act_free[s] = [em.pe_now()]
            slot_dead[s] = em.pe_now()
            if e_ + 2 < E:
                emit_exp_load(e_ + 2)

        def do_T2(tt):
            b = tb[tt % 2]
            em.wait("tensor", b.free)
            b.free = []
            em.wait("tensor", ev_s2[tt])
            last = None
            for dc in range(NDC):
                last = em.emit("tensor", lambda e, o=b.ap[:, dc * 128:(dc + 1) * 128],
                               i=h2_tm_slot[tt % 3][:, dc * 128:(dc + 1) * 128]:
                               e.transpose(o, i, ident_sb[:, :]),
                               inc="pe" if dc == NDC - 1 else None)
            h2_free[tt % 3] = [last]
            em.wait("vector", last)
            ev = em.emit("vector", lambda e,
                         o=h2_fm.rearrange("p (c t) -> p c t", c=NDC)[:, :, tt * 128:(tt + 1) * 128],
                         i=b.ap[:, :].rearrange("p (c w) -> p c w", c=NDC):
                         e.tensor_copy(o, i), inc="dve")
            h2_ready[tt] = ev
            b.free = [ev]

        def do_router(tt):
            brt = acquire_fb()
            em.wait("tensor", d_gwt)
            em.wait("tensor", h2_ready[tt])
            rlast = None
            for dc in range(NDC):
                rlast = em.emit("tensor", lambda e, o=brt.ap[:, 0:16],
                                l=h2_fm[:, dc * 2048 + tt * 128: dc * 2048 + tt * 128 + 128],
                                r=gwt_sb[:, dc * E:(dc + 1) * E],
                                s=(dc == 0), p=(dc == NDC - 1):
                                e.matmul(o, l, r, start=s, stop=p),
                                inc="pe" if dc == NDC - 1 else None)
            em.wait("scalar", rlast)
            evsig = em.emit("scalar", lambda e,
                            o=escore.rearrange("p (ee t) -> p ee t", ee=E)[:, :, tt:tt + 1],
                            i=brt.ap[:, 0:16].rearrange("p (a ee) -> p ee a", a=1):
                            e.activation(o, i, AF.Sigmoid), inc="act")
            sig_evs.append(evsig)
            brt.free = [evsig]

        for tc in range(4):
            for tl in range(4):
                tt = 4 * tc + tl
                do_T2(tt)
                if tt + 3 < NT:
                    lnv2_a(tt + 3)
                if tt + 2 < NT:
                    lnv2_b(tt + 2)
                do_router(tt)
            emit_gu(0, tc)

        def eb(t, e_):
            return t[:, e_ * 16:(e_ + 1) * 16]

        sfc3 = sfc.rearrange("p (g f) -> p g f", g=4)
        pt3 = ptree.rearrange("p (g f) -> p g f", g=4)
        spacer = lambda e: e.tensor_scalar_mul(spc, gs[:, 0:16], 1.0)
        comb_ev_box = [None]

        def emit_ge(d):
            w = (16 - d) * 16
            em.emit("vector", lambda e, o=cmpb[d % 3], x1=tmpsc[:, 0:w],
                    x2=tmpsc[:, d * 16:d * 16 + w], w=w:
                    e.tensor_tensor(o[:, 0:w], x1, x2, op=ALU.is_ge))

        def emit_rank_e(d_lo, d_hi):
            for d in range(d_lo, d_hi):
                w = (16 - d) * 16
                em.emit("vector", lambda e, x=rank_e[:, d * 16:d * 16 + w],
                        c=cmpb[d % 3][:, 0:w]: e.tensor_add(x, x, c))
                if d + 2 <= 15:
                    emit_ge(d + 2)
                else:
                    em.emit("vector", spacer)
                em.emit("vector", lambda e, x=rank_e[:, 0:w],
                        c=cmpb[d % 3][:, 0:w]: e.tensor_sub(x, x, c))
                em.emit("vector", spacer)

        def emit_tree_seg(seg):
            if seg == 0:
                em.wait("vector", sig_evs)
                em.wait("vector", d_ebb)
                for e_ in range(E):
                    em.emit("vector", lambda e, o=eb(sfc, e_), i=eb(escore, e_),
                            c=ebb_sb[:, e_:e_ + 1]: e.tensor_scalar_add(o, i, c))
                pairs4 = [(0, 1), (0, 2), (0, 3), (1, 2), (1, 3), (2, 3)]
                for pi, (a1, a2) in enumerate(pairs4):
                    em.emit("vector", lambda e, o=pt3[:, :, pi * 16:(pi + 1) * 16],
                            x1=sfc3[:, :, a1 * 16:(a1 + 1) * 16],
                            x2=sfc3[:, :, a2 * 16:(a2 + 1) * 16]:
                            e.tensor_add(o, x1, x2))
                for s1, s2, dst in ((0, 1, 6), (2, 3, 7), (4, 5, 8)):
                    em.emit("vector", lambda e, o=pt3[:, :, dst * 16:(dst + 1) * 16],
                            x1=pt3[:, :, s1 * 16:(s1 + 1) * 16],
                            x2=pt3[:, :, s2 * 16:(s2 + 1) * 16]:
                            e.tensor_max(o, x1, x2))
                em.emit("vector", spacer)
                em.emit("vector", lambda e, o=pt3[:, :, 144:160], x1=pt3[:, :, 96:112],
                        x2=pt3[:, :, 112:128]: e.tensor_max(o, x1, x2))
                em.emit("vector", spacer)
                em.emit("vector", lambda e, o=gs.rearrange("p (g f) -> p g f", g=4),
                        x1=pt3[:, :, 144:160], x2=pt3[:, :, 128:144]:
                        e.tensor_max(o, x1, x2))
                for gi in range(4):
                    em.emit("vector", lambda e, o=rank_g[:, gi * 16:(gi + 1) * 16],
                            v=float(3 - gi): e.memset(o, v))
            elif seg == 1:
                gcmp = [cmpb[i][:, 0:48] for i in range(3)]
                for d in range(1, 4):
                    w = (4 - d) * 16
                    em.emit("vector", lambda e, o=gcmp[d - 1][:, 0:w], x1=gs[:, 0:w],
                            x2=gs[:, d * 16:d * 16 + w], w=w:
                            e.tensor_tensor(o[:, 0:w], x1, x2, op=ALU.is_ge))
                for d in range(1, 4):
                    w = (4 - d) * 16
                    em.emit("vector", lambda e, x=rank_g[:, d * 16:d * 16 + w],
                            c=gcmp[d - 1][:, 0:w]: e.tensor_add(x, x, c))
                    em.emit("vector", spacer)
                    em.emit("vector", lambda e, x=rank_g[:, 0:w],
                            c=gcmp[d - 1][:, 0:w]: e.tensor_sub(x, x, c))
                    em.emit("vector", spacer)
                em.emit("vector", lambda e: e.tensor_single_scalar(gmask, rank_g, 2.0, ALU.is_lt))
                em.emit("vector", spacer)
                gm3 = gmask.rearrange("p (g f) -> p g f", g=4)
                tm3 = tmpsc.rearrange("p (g f) -> p g f", g=4)
                for a in range(4):
                    em.emit("vector", lambda e, o=tm3[:, :, a * 16:(a + 1) * 16],
                            x1=sfc3[:, :, a * 16:(a + 1) * 16], x2=gm3:
                            e.tensor_mul(o, x1, x2))
                for e_ in range(E):
                    em.emit("vector", lambda e, o=eb(rank_e, e_), v=float(15 - e_):
                            e.memset(o, v))
                emit_ge(1)
                emit_ge(2)
                emit_rank_e(1, 5)
            elif seg == 2:
                emit_rank_e(5, 11)
            else:
                emit_rank_e(11, 16)
                em.emit("vector", lambda e: e.tensor_single_scalar(sfc, rank_e, 4.0, ALU.is_lt))
                em.emit("vector", spacer)
                em.emit("vector", lambda e: e.tensor_mul(tmpsc, escore, sfc))
                em.emit("vector", spacer)
                tm8 = tmpsc.rearrange("p (h f) -> p h f", h=8)
                dt8 = dtree[:, 0:128].rearrange("p (h f) -> p h f", h=8)
                em.emit("vector", lambda e: e.tensor_add(dt8, tm8[:, :, 0:16], tm8[:, :, 16:32]))
                em.emit("vector", spacer)
                dt4i = dtree[:, 0:128].rearrange("p (h f) -> p h f", h=4)
                dt4 = dtree[:, 128:192].rearrange("p (h f) -> p h f", h=4)
                em.emit("vector", lambda e: e.tensor_add(dt4, dt4i[:, :, 0:16], dt4i[:, :, 16:32]))
                em.emit("vector", spacer)
                dt2i = dtree[:, 128:192].rearrange("p (h f) -> p h f", h=2)
                em.emit("vector", lambda e, o=dtree[:, 192:224].rearrange("p (h f) -> p h f", h=2):
                        e.tensor_add(o, dt2i[:, :, 0:16], dt2i[:, :, 16:32]))
                em.emit("vector", spacer)
                em.emit("vector", lambda e: e.tensor_add(den, dtree[:, 192:208], dtree[:, 208:224]))
                em.emit("vector", spacer)
                em.emit("vector", lambda e: e.tensor_scalar_add(den, den, 1e-20))
                em.emit("vector", spacer)
                em.emit("vector", lambda e: e.reciprocal(rden, den))
                em.emit("vector", spacer)
                for e_ in range(E):
                    comb_ev_box[0] = em.emit("vector",
                                             lambda e, o=eb(combine, e_), i=eb(tmpsc, e_):
                                             e.tensor_mul(o, i, rden), inc="dve")

        for tc in range(4):
            emit_gu(1, tc)
            emit_tree_seg(tc)
        for k in range(E // 2):
            if k > 0:
                for tc in range(4):
                    emit_gu(2 * k, tc)
                for tc in range(4):
                    emit_gu(2 * k + 1, tc)
            emit_down(2 * k, comb_ev_box[0])
            emit_down(2 * k + 1, comb_ev_box[0])

        yv = y_d.rearrange("(tt p) d -> p tt d", p=128)
        for tt in range(NT):
            qeng = "sync" if tt % 2 == 0 else "gpsimd"
            em.wait(qeng, x_final[tt])
            em.emit(qeng, lambda e, o=yv[:, tt:tt + 1, :],
                    i=x_tm[:, tt * D:(tt + 1) * D].rearrange("p (a d) -> p a d", a=1):
                    e.dma_start(out=o, in_=i), inc="st", n=16)
        em.wait("sync", ("st", em.ctr["st"]))

        em.check_deadlock()

        with nc.Block() as block:
            @block.tensor
            def _(t):
                for f in em.ops["tensor"]:
                    f(t)

            @block.vector
            def _(v):
                for f in em.ops["vector"]:
                    f(v)

            @block.scalar
            def _(sx):
                for f in em.ops["scalar"]:
                    f(sx)

            @block.gpsimd
            def _(gp):
                for f in em.ops["gpsimd"]:
                    f(gp)

            @block.sync
            def _(sy):
                for f in em.ops["sync"]:
                    f(sy)

    return nc


_NC_CACHE = None


def kernel(L0_attn_x, L0_attn_Wq, L0_attn_bq, L0_attn_Wk, L0_attn_bk,
           L0_attn_Wv, L0_attn_bv, L0_attn_Wo, L0_attn_bo,
           L0_ffn_gate_weight, L0_ffn_e_bias, L0_ffn_gate_proj,
           L0_ffn_up_proj, L0_ffn_down_proj):
    global _NC_CACHE
    x = np.ascontiguousarray(np.asarray(L0_attn_x, dtype=np.float32))
    f32 = lambda a: np.ascontiguousarray(np.asarray(a, dtype=np.float32))

    wqt = f32(np.asarray(L0_attn_Wq).T)
    wkt = f32(np.asarray(L0_attn_Wk).T)
    wvt = f32(np.asarray(L0_attn_Wv).T)
    wot = f32(np.asarray(L0_attn_Wo).T)
    gwt = f32(np.asarray(L0_ffn_gate_weight).T)
    gT = f32(np.asarray(L0_ffn_gate_proj).transpose(0, 2, 1))
    uT = f32(np.asarray(L0_ffn_up_proj).transpose(0, 2, 1))
    dT = f32(np.asarray(L0_ffn_down_proj).transpose(0, 2, 1))

    bq = f32(L0_attn_bq)
    bk = f32(L0_attn_bk)
    bv = f32(L0_attn_bv)
    bo = f32(L0_attn_bo)
    bias_cm = np.concatenate([bq.reshape(8, 128).T, bk.reshape(8, 128).T],
                             axis=1)
    bias_cm = f32(bias_cm)
    brows = f32(np.concatenate([bk, bv, bo]))
    ebb = f32(np.broadcast_to(np.asarray(L0_ffn_e_bias, np.float32)[None, :], (128, E)))
    mask = f32(np.triu(np.ones((128, 128), np.float32)))
    ident = f32(np.eye(128, dtype=np.float32))

    if _NC_CACHE is None:
        _NC_CACHE = build_program()
    nc = _NC_CACHE

    shared = dict(wqt=wqt, wkt=wkt, wvt=wvt, wot=wot, gwt=gwt,
                  gateT=gT, upT=uT, downT=dT, bias_cm=bias_cm, brows=brows,
                  ebb=ebb, mask=mask, ident=ident)
    in_maps = [dict(shared, x=np.ascontiguousarray(x[i])) for i in range(8)]
    res = run_bass_kernel_spmd(nc, in_maps, core_ids=list(range(8)))
    out = np.stack([res.results[i]["y"] for i in range(8)], axis=0)
    return out.astype(np.float32)


# version 18 (final)
# speedup vs baseline: 1.0155x; 1.0050x over previous
import numpy as np
from contextlib import ExitStack

import concourse.bass as bass
import concourse.mybir as mybir
from concourse.bass_utils import run_bass_kernel_spmd

F32 = mybir.dt.float32
BF16 = mybir.dt.bfloat16
U8 = mybir.dt.uint8
AF = mybir.ActivationFunctionType
ALU = mybir.AluOpType

D = 1024
T = 2048
H = 16
NDC = 8
NT = 16
NG = 4
E = 16
EPS_LN = 1e-5
EPS_Z = 1e-6

ENGS = ("tensor", "vector", "scalar", "gpsimd", "sync")


class Em:
    def __init__(self, nc, sems):
        self.nc = nc
        self.sems = sems
        self.ops = {k: [] for k in ENGS}
        self.ctr = {k: 0 for k in sems}
        self.waited = {k: {} for k in ENGS}
        self.trace = {k: [] for k in ENGS}

    def wait(self, eng, dep):
        if dep is None:
            return
        if isinstance(dep, list):
            for d in dep:
                self.wait(eng, d)
            return
        sem, val = dep
        if val <= 0 or self.waited[eng].get(sem, 0) >= val:
            return
        self.waited[eng][sem] = val
        self.trace[eng].append(("w", sem, val))
        self.ops[eng].append(lambda e, s=self.sems[sem], v=val: e.wait_ge(s, v))

    def emit(self, eng, fn, inc=None, n=1):
        if inc is None:
            self.ops[eng].append(fn)
            return None
        self.ctr[inc] += n
        val = self.ctr[inc]
        self.trace[eng].append(("i", inc, n))
        self.ops[eng].append(
            lambda e, f=fn, s=self.sems[inc], k=n: f(e).then_inc(s, k))
        return (inc, val)

    def check_deadlock(self):
        pos = {k: 0 for k in ENGS}
        cnt = {k: 0 for k in self.ctr}
        progress = True
        while progress:
            progress = False
            for eng in ENGS:
                tr = self.trace[eng]
                while pos[eng] < len(tr):
                    kind, sem, v = tr[pos[eng]]
                    if kind == "w" and cnt[sem] < v:
                        break
                    if kind == "i":
                        cnt[sem] += v
                    pos[eng] += 1
                    progress = True
        stuck = {e: (pos[e], len(self.trace[e]), self.trace[e][pos[e]]
                     if pos[e] < len(self.trace[e]) else None) for e in ENGS
                 if pos[e] < len(self.trace[e])}
        if stuck:
            raise RuntimeError(f"deadlock: {stuck}; cnt={cnt}")

    def pe_now(self):
        return ("pe", self.ctr["pe"])


class Bank:
    def __init__(self, ap):
        self.ap = ap
        self.free = []


def build_program():
    nc = bass.Bass("TRN2", target_bir_lowering=False)

    x_d = nc.dram_tensor("x", [T, D], F32, kind="ExternalInput").ap()
    wqt_d = nc.dram_tensor("wqt", [D, D], F32, kind="ExternalInput").ap()
    wkt_d = nc.dram_tensor("wkt", [D, D], F32, kind="ExternalInput").ap()
    wvt_d = nc.dram_tensor("wvt", [D, D], F32, kind="ExternalInput").ap()
    wot_d = nc.dram_tensor("wot", [D, D], F32, kind="ExternalInput").ap()
    gwt_d = nc.dram_tensor("gwt", [D, E], F32, kind="ExternalInput").ap()
    gT_d = nc.dram_tensor("gateT", [E, D, 512], F32, kind="ExternalInput").ap()
    uT_d = nc.dram_tensor("upT", [E, D, 512], F32, kind="ExternalInput").ap()
    dT_d = nc.dram_tensor("downT", [E, 512, D], F32, kind="ExternalInput").ap()
    bias_cm_d = nc.dram_tensor("bias_cm", [128, 16], F32, kind="ExternalInput").ap()
    brows_d = nc.dram_tensor("brows", [3 * D], F32, kind="ExternalInput").ap()
    ebb_d = nc.dram_tensor("ebb", [128, E], F32, kind="ExternalInput").ap()
    mask_d = nc.dram_tensor("mask", [128, 128], F32, kind="ExternalInput").ap()
    ident_d = nc.dram_tensor("ident", [128, 128], F32, kind="ExternalInput").ap()
    y_d = nc.dram_tensor("y", [T, D], F32, kind="ExternalOutput").ap()

    x_tm = nc.alloc_sbuf_tensor("x_tm", [128, NT * D], F32).ap()

    arena_w = nc.alloc_sbuf_tensor("arena_w", [128, 65536], U8).ap()
    wqt_sb = arena_w[:, 0:16384].bitcast(BF16)
    wkt_sb = arena_w[:, 16384:32768].bitcast(BF16)
    wvt_sb = arena_w[:, 32768:49152].bitcast(BF16)
    wot_sb = arena_w[:, 49152:65536].bitcast(BF16)
    moe_slot = []
    for s in range(2):
        base = s * 24576
        moe_slot.append(dict(
            g=arena_w[:, base:base + 8192].bitcast(BF16),
            u=arena_w[:, base + 8192:base + 16384].bitcast(BF16),
            d=arena_w[:, base + 16384:base + 24576].bitcast(BF16)))
    rbase = 49152

    def rview(off, cols):
        return arena_w[:, rbase + off * 4: rbase + (off + cols) * 4].bitcast(F32)
    escore = rview(0, 256)
    sfc = rview(256, 256)
    tmpsc = rview(512, 256)
    rank_e = rview(768, 256)
    combine = rview(1024, 256)
    gs = rview(1280, 64)
    rank_g = rview(1344, 64)
    gmask = rview(1408, 64)
    ptree = rview(1472, 640)
    cmpb = [rview(2112 + i * 256, 256) for i in range(3)]
    dtree = rview(2880, 224)
    den = rview(3104, 16)
    rden = rview(3120, 16)
    spc = rview(3136, 16)

    arena_h = nc.alloc_sbuf_tensor("arena_h", [128, 32768], U8).ap()
    ln_tm_slot = [arena_h[:, 0:2048].bitcast(BF16),
                  arena_h[:, 2048:4096].bitcast(BF16),
                  arena_h[:, 4096:6144].bitcast(BF16),
                  arena_h[:, 6144:8192].bitcast(BF16)]
    attn_tm_slot = [arena_h[:, 4096:6144].bitcast(BF16),
                    arena_h[:, 6144:8192].bitcast(BF16)]
    h1_fm = arena_h[:, 8192:16384].bitcast(BF16)
    attn_fm = arena_h[:, 16384:24576].bitcast(BF16)
    h2_fm = arena_h[:, 0:32768].bitcast(BF16)
    brows_sb = arena_h[:, 24576:30720].bitcast(BF16)
    A_m = arena_h[:, 30720:31744].bitcast(BF16)
    mask_sb = arena_h[:, 31744:32256].bitcast(F32)
    ones_row = arena_h[:, 32256:32512].bitcast(BF16)
    bias_cm = arena_h[:, 32512:32576].bitcast(F32)
    zr_sb = arena_h[:, 32576:32640].bitcast(F32)

    arena_p = nc.alloc_sbuf_tensor("arena_p", [128, 33024], U8).ap()
    q_grp = arena_p[:, 0:8192].bitcast(BF16)
    k_grp = arena_p[:, 8192:16384].bitcast(BF16)
    ktm_grp = arena_p[:, 16384:24576].bitcast(BF16)
    vaug_grp = arena_p[:, 24576:33024].bitcast(BF16)
    act_slot = [arena_p[:, 0:16384].bitcast(BF16),
                arena_p[:, 16384:32768].bitcast(BF16)]
    ln1_junk = arena_p[:, 0:4096].bitcast(F32)
    ln2_junk = arena_p[:, 16384:20480].bitcast(F32)

    scr = nc.alloc_sbuf_tensor("scr", [128, 8192], U8).ap()
    m_scr = [scr[:, 0:2048].bitcast(F32), scr[:, 4096:6144].bitcast(F32)]
    e_scr = [scr[:, 2048:4096].bitcast(F32), scr[:, 6144:8192].bitcast(F32)]
    mask4 = scr[:, 6144:7168].bitcast(BF16)
    A_m2 = scr[:, 7168:8192].bitcast(BF16)
    sg_scr = [scr[:, 0:2048].bitcast(F32), scr[:, 2048:4096].bitcast(F32)]
    h2_tm_slot0 = scr[:, 4096:6144].bitcast(BF16)

    xtra = nc.alloc_sbuf_tensor("xtra", [128, 2048], U8).ap()
    A_m_w1 = xtra[:, 0:1024].bitcast(BF16)
    A_m2_w1 = xtra[:, 1024:2048].bitcast(BF16)
    h2_tm_slot1 = xtra[:, 0:2048].bitcast(BF16)
    h2_tm_slot2 = scr[:, 6144:8192].bitcast(BF16)
    h2_tm_slot = [h2_tm_slot0, h2_tm_slot1, h2_tm_slot2]

    spc_a = nc.alloc_sbuf_tensor("spc_a", [128, 16], F32).ap()
    S_f32 = nc.alloc_sbuf_tensor("S_f32", [128, 528], F32).ap()
    S_bf = nc.alloc_sbuf_tensor("S_bf", [128, 528], BF16).ap()
    ident_sb = nc.alloc_sbuf_tensor("ident_sb", [128, 128], BF16).ap()
    ebb_sb = nc.alloc_sbuf_tensor("ebb_sb", [128, E], F32).ap()
    gwt_sb = nc.alloc_sbuf_tensor("gwt_sb", [128, NDC * E], BF16).ap()
    stats = nc.alloc_sbuf_tensor("stats", [128, 120], F32).ap()

    fb = [Bank(nc.alloc_psum_tensor(f"pb{i}", [128, 512], F32).ap()) for i in range(6)]
    tb = [Bank(nc.alloc_psum_tensor(f"tbk{i}", [128, 1024], BF16).ap()) for i in range(2)]

    with ExitStack() as stack:
        sems = {}
        for name in ("lx0", "lx1", "lx2", "lx3", "lx4", "lx5", "ldc",
                     "lwa", "lwq", "lwk", "lwv", "lwo", "lw0", "lw1",
                     "pe", "dve", "act", "gp", "st"):
            sems[name] = stack.enter_context(nc.semaphore(name))
        em = Em(nc, sems)

        d_x_dep = [None] * NT
        xv = x_d.rearrange("(tt p) d -> p tt d", p=128)

        xsems = ["lx0", "lx1", "lx2", "lx3", "lx4", "lx5"]
        xsem_i = [0]

        def load_x(t0, ntl):
            sem = xsems[xsem_i[0]]
            xsem_i[0] += 1
            ev = em.emit("sync", lambda e: e.dma_start(
                out=x_tm[:, t0 * D:(t0 + ntl) * D].rearrange("p (tt d) -> p tt d", d=D),
                in_=xv[:, t0:t0 + ntl, :]), inc=sem, n=16)
            for t in range(t0, t0 + ntl):
                d_x_dep[t] = ev
        em.emit("sync", lambda e: e.dma_start(out=mask_sb[:, :], in_=mask_d), inc="ldc", n=16)
        em.emit("sync", lambda e: e.dma_start(out=bias_cm[:, :], in_=bias_cm_d), inc="ldc", n=16)
        load_x(0, 1)
        load_x(1, 1)
        load_x(2, 2)
        for g in range(1, NG):
            load_x(4 * g, 4)
        em.emit("sync", lambda e: e.dma_start(out=ebb_sb[:, :], in_=ebb_d), inc="ldc", n=16)
        d_mask = d_bcm = d_ebb = ("ldc", 48)

        def wload(sem, out_sb, in_ap):
            return em.emit("gpsimd", lambda e, o=out_sb, i=in_ap: e.dma_start(out=o, in_=i),
                           inc=sem, n=16)

        wload("lwa", ident_sb[:, :], ident_d)
        wload("lwa", brows_sb[0:1, :], brows_d[None, :])
        d_ident = d_brows = ("lwa", 32)
        d_wq = wload("lwq", wqt_sb.rearrange("p (c o) -> p c o", c=NDC),
                     wqt_d.rearrange("(c p) o -> p c o", p=128))
        d_wk = wload("lwk", wkt_sb.rearrange("p (c o) -> p c o", c=NDC),
                     wkt_d.rearrange("(c p) o -> p c o", p=128))
        d_wv = wload("lwv", wvt_sb.rearrange("p (c o) -> p c o", c=NDC),
                     wvt_d.rearrange("(c p) o -> p c o", p=128))
        wload("lwo", wot_sb.rearrange("p (c o) -> p c o", c=NDC),
              wot_d.rearrange("(c p) o -> p c o", p=128))
        wload("lwo", gwt_sb.rearrange("p (c o) -> p c o", c=NDC),
              gwt_d.rearrange("(c p) o -> p c o", p=128))
        d_wo = d_gwt = ("lwo", 32)

        em.emit("vector", lambda e: e.memset(ones_row[0:1, :], 1.0))
        em.emit("vector", lambda e: e.memset(S_f32[:, :], 0.0))
        em.emit("vector", lambda e: e.memset(S_bf[:, :], 0.0))
        em.emit("vector", lambda e: e.memset(
            vaug_grp.rearrange("p (g c) -> p g c", c=66)[:, :, 64:65], 1.0))
        d_const = em.emit("vector", lambda e: e.memset(
            vaug_grp.rearrange("p (g c) -> p g c", c=66)[:, :, 65:66], 0.0), inc="dve")

        bank_i = [0]

        def acquire_fb():
            b = fb[bank_i[0] % 6]
            bank_i[0] += 1
            em.wait("tensor", b.free)
            b.free = []
            return b

        junk_last = {}

        def ln_stats(xt, st, junk, x_dep, junk_dep, jkey):
            em.wait("scalar", x_dep)
            em.wait("scalar", junk_dep)
            em.wait("scalar", junk_last.get(jkey))
            ev_c = em.emit("scalar", lambda e, o=junk, i=xt, a=st[:, 0:1]:
                           e.activation(o, i, AF.Copy, accum_out=a), inc="act")
            em.wait("scalar", ev_c)
            ev = em.emit("scalar", lambda e, o=junk, i=xt, a=st[:, 1:2]:
                         e.activation(o, i, AF.Square, accum_out=a), inc="act")
            junk_last[jkey] = ev
            return ev

        smalls_last = {}
        sqrt_last = {}

        def ln_smalls(st, ev_stats, skey):
            em.wait("vector", ev_stats)
            em.wait("vector", smalls_last.get(skey))
            em.emit("vector", lambda e, o=st[:, 2:3], i=st[:, 0:1]:
                    e.tensor_scalar_mul(o, i, 1.0 / D))
            em.emit("vector", lambda e, o=st[:, 3:4], i=st[:, 1:2]:
                    e.tensor_scalar(o, i, 1.0 / D, EPS_LN, op0=ALU.mult, op1=ALU.add))
            em.emit("vector", lambda e, o=st[:, 4:5], i=st[:, 2:3]: e.tensor_mul(o, i, i))
            em.emit("vector", lambda e, o=st[:, 9:10], i=st[:, 0:1]:
                    e.tensor_scalar_mul(o, i, 1.0))
            em.emit("vector", lambda e, o=st[:, 5:6], i=st[:, 3:4], m2=st[:, 4:5]:
                    e.tensor_sub(o, i, m2))
            em.emit("vector", lambda e, o=st[:, 9:10], i=st[:, 0:1]:
                    e.tensor_scalar_mul(o, i, 1.0))
            ev = em.emit("vector", lambda e, o=st[:, 6:7], i=st[:, 5:6]:
                         e.reciprocal(o, i), inc="dve")
            smalls_last[skey] = ev
            return ev

        def ln_sqrt(st, ev_v, skey):
            em.wait("scalar", ev_v)
            em.wait("scalar", sqrt_last.get(skey))
            ev = em.emit("scalar", lambda e, o=st[:, 7:8], i=st[:, 6:7]:
                         e.activation(o, i, AF.Sqrt), inc="act")
            sqrt_last[skey] = ev
            return ev

        def ln_fin(xt, out_ap, st, ev_r, deps_out):
            em.wait("vector", ev_r)
            em.wait("vector", deps_out)
            return em.emit("vector", lambda e, o=out_ap, i=xt, m=st[:, 2:3], r=st[:, 7:8]:
                           e.tensor_scalar(o, i, m, r, op0=ALU.subtract, op1=ALU.mult),
                           inc="dve")

        exp_loads = [None] * E
        slot_dead = [None, None]

        def emit_exp_load(e_):
            s = e_ % 2
            sem = "lw0" if s == 0 else "lw1"
            em.wait("gpsimd", slot_dead[s])
            wload(sem, moe_slot[s]["g"].rearrange("p (c i) -> p c i", c=NDC),
                  gT_d[e_].rearrange("(c p) i -> p c i", p=128))
            wload(sem, moe_slot[s]["u"].rearrange("p (c i) -> p c i", c=NDC),
                  uT_d[e_].rearrange("(c p) i -> p c i", p=128))
            exp_loads[e_] = wload(sem, moe_slot[s]["d"].rearrange("p (c o) -> p c o", c=4),
                                  dT_d[e_].rearrange("(c p) o -> p c o", p=128))

        qk_free = []
        ktm_free = []
        vaug_free = []
        attn_tm_free = [[], []]
        lnslot_free = [[], [], [], []]
        zr_last = [None]
        scr_free = [[], []]
        am_free = [[], []]
        mirror_ev = [None, None]
        x_tile_y1 = [None] * NT
        ln_s_ev = [None] * NT
        gp_a3_final = None

        def tm_tile_to_fm(tm_ap, fm_ap, tl, bslot, ready_dep, evac_eng="vector",
                          bank=None):
            b = bank if bank is not None else tb[bslot]
            bap = b.ap if bank is None else b.ap.bitcast(BF16)
            em.wait("tensor", b.free)
            b.free = []
            em.wait("tensor", d_ident)
            last = None
            for dc in range(NDC):
                em.wait("tensor", ready_dep)
                last = em.emit("tensor", lambda e, o=bap[:, dc * 128:(dc + 1) * 128],
                               i=tm_ap[:, dc * 128:(dc + 1) * 128]:
                               e.transpose(o, i, ident_sb[:, :]),
                               inc="pe" if dc == NDC - 1 else None)
            oap = fm_ap.rearrange("p (c t) -> p c t", c=NDC)[:, :, tl * 128:(tl + 1) * 128]
            iap = bap[:, :].rearrange("p (c w) -> p c w", c=NDC)
            if evac_eng == "vector":
                em.wait("vector", last)
                ev = em.emit("vector", lambda e, o=oap, i=iap:
                             e.tensor_copy(o, i), inc="dve")
            else:
                em.wait("scalar", last)
                ev = em.emit("scalar", lambda e, o=oap, i=iap:
                             e.copy(o, i), inc="act")
            b.free = [ev]
            return last, ev

        ln1_evr = [None] * 4

        def ln1_vs_tile(g, tl):
            tt = 4 * g + tl
            st = stats[:, (tt % 4) * 10:(tt % 4) * 10 + 10]
            xt = x_tm[:, tt * D:(tt + 1) * D]
            ev_st = ln_stats(xt, st, ln1_junk, d_x_dep[tt], qk_free, 'j1')
            ev_v = ln_smalls(st, ev_st, ('a', tt % 4))
            ln1_evr[tl] = ln_sqrt(st, ev_v, ('a', tt % 4))

        def ln1_fin_tile(g, tl):
            tt = 4 * g + tl
            st = stats[:, (tt % 4) * 10:(tt % 4) * 10 + 10]
            xt = x_tm[:, tt * D:(tt + 1) * D]
            guards = lnslot_free[tl] + (attn_tm_free[tl - 2] if tl >= 2 else [])
            ln_s_ev[tt] = ln_fin(xt, ln_tm_slot[tl][:, :], st, ln1_evr[tl], guards)

        def ln1_T_tile(g, tl):
            tt = 4 * g + tl
            pe_last, evac = tm_tile_to_fm(ln_tm_slot[tl], h1_fm, tl, tl % 2,
                                          ln_s_ev[tt])
            lnslot_free[tl] = [pe_last]
            return evac

        ev_s2 = [None] * NT
        h2_free = [[], [], []]
        h2_ready = [None] * NT
        sig_evs = []
        ln2_evr = [None] * NT

        def lnv2_a(tt):
            xt = x_tm[:, tt * D:(tt + 1) * D]
            st = stats[:, 40 + (tt % 8) * 10: 40 + (tt % 8) * 10 + 10]
            ev_st = ln_stats(xt, st, ln2_junk, x_tile_y1[tt], None, 'j2')
            ev_v = ln_smalls(st, ev_st, ('b', tt % 8))
            ln2_evr[tt] = ln_sqrt(st, ev_v, ('b', tt % 8))

        def lnv2_b(tt):
            xt = x_tm[:, tt * D:(tt + 1) * D]
            st = stats[:, 40 + (tt % 8) * 10: 40 + (tt % 8) * 10 + 10]
            extra = [gp_a3_final] if tt % 3 != 1 else []
            ev_s2[tt] = ln_fin(xt, h2_tm_slot[tt % 3][:, :], st, ln2_evr[tt],
                               h2_free[tt % 3] + extra)

        for tl in range(4):
            ln1_vs_tile(0, tl)
            if tl >= 1:
                ln1_fin_tile(0, tl - 1)
        ln1_fin_tile(0, 3)
        h1fm_ready = [ln1_T_tile(0, tl) for tl in range(4)]

        for g in range(NG):
            def phi_evac(bank, last_pe, bias_ap, out_ap, si, out_free):
                em.wait("vector", last_pe)
                em.wait("vector", scr_free[si])
                if bias_ap is not None:
                    ev_m = em.emit("vector", lambda e, o=m_scr[si], i=bank.ap[:, :], ba=bias_ap:
                                   e.tensor_scalar(o, i, ba, 0.0, op0=ALU.add, op1=ALU.min),
                                   inc="dve")
                else:
                    ev_m = em.emit("vector", lambda e, o=m_scr[si], i=bank.ap[:, :]:
                                   e.tensor_scalar_min(o, i, 0.0), inc="dve")
                em.wait("scalar", ev_m)
                ev_e = em.emit("scalar", lambda e, o=e_scr[si], i=m_scr[si]:
                               e.activation(o, i, AF.Exp), inc="act")
                em.wait("scalar", last_pe)
                if bias_ap is not None:
                    em.wait("scalar", d_bcm)
                    ev_r = em.emit("scalar", lambda e, o=m_scr[si], i=bank.ap[:, :], ba=bias_ap:
                                   e.activation(o, i, AF.Relu, bias=ba), inc="act")
                else:
                    ev_r = em.emit("scalar", lambda e, o=m_scr[si], i=bank.ap[:, :]:
                                   e.activation(o, i, AF.Relu), inc="act")
                bank.free = [ev_r]
                em.wait("gpsimd", ev_e)
                em.wait("gpsimd", ev_r)
                em.wait("gpsimd", out_free)
                ev = em.emit("gpsimd", lambda e, o=out_ap, a=e_scr[si], bb=m_scr[si]:
                             e.tensor_add(o, a, bb), inc="gp")
                scr_free[si] = [ev]
                return ev

            q_ready, k_ready = [], []
            for which, w_sb, w_dep, bcol, out_grp, rdy in (
                    ("q", wqt_sb, d_wq, 0, q_grp, q_ready),
                    ("k", wkt_sb, d_wk, 8, k_grp, k_ready)):
                for oc in range(NDC):
                    b = acquire_fb()
                    em.wait("tensor", w_dep)
                    em.wait("tensor", h1fm_ready)
                    last = None
                    for dc in range(NDC):
                        last = em.emit("tensor", lambda e, o=b.ap[:, :],
                                       l=w_sb[:, dc * 1024 + oc * 128: dc * 1024 + oc * 128 + 128],
                                       r=h1_fm[:, dc * 512:(dc + 1) * 512],
                                       s=(dc == 0), p=(dc == NDC - 1):
                                       e.matmul(o, l, r, start=s, stop=p),
                                       inc="pe" if dc == NDC - 1 else None)
                    em.wait("vector", d_bcm)
                    bap = bias_cm[:, bcol + oc: bcol + oc + 1]
                    ev = phi_evac(b, last, bap, out_grp[:, oc * 512:(oc + 1) * 512],
                                  oc % 2, qk_free)
                    rdy.append(ev)
            qk_free = []
            if g == NG - 1:
                slot_dead[0] = em.pe_now()
                emit_exp_load(0)

            v_ready = []
            for tl in range(4):
                evs_tl = []
                for dh in range(2):
                    b = acquire_fb()
                    em.wait("tensor", [d_wv, d_brows])
                    em.wait("tensor", h1fm_ready)
                    for dc in range(NDC):
                        em.emit("tensor", lambda e, o=b.ap[:, :],
                                l=h1_fm[:, dc * 512 + tl * 128: dc * 512 + tl * 128 + 128],
                                r=wvt_sb[:, dc * 1024 + dh * 512: dc * 1024 + dh * 512 + 512],
                                s=(dc == 0): e.matmul(o, l, r, start=s, stop=False))
                    last = em.emit("tensor", lambda e, o=b.ap[:, :],
                                   r=brows_sb[0:1, D + dh * 512: D + dh * 512 + 512]:
                                   e.matmul(o, ones_row[0:1, :], r, start=False, stop=True),
                                   inc="pe")
                    em.wait("vector", last)
                    em.wait("vector", vaug_free)
                    ev = em.emit("vector", lambda e,
                                 o=vaug_grp[:, tl * 1056 + 528 * dh: tl * 1056 + 528 * dh + 528]
                                 .rearrange("p (h c) -> p h c", c=66)[:, :, 0:64],
                                 i=b.ap[:, :].rearrange("p (h c) -> p h c", c=64):
                                 e.tensor_copy(o, i), inc="dve")
                    b.free = [ev]
                    evs_tl.append(ev)
                v_ready.append(evs_tl)
            vaug_free = []
            if g == NG - 1:
                slot_dead[1] = em.pe_now()
                emit_exp_load(1)
                gp_a3_final = ("gp", em.ctr["gp"])

            ktm_ready = []
            for tl in range(4):
                b = tb[tl % 2]
                em.wait("tensor", b.free)
                b.free = []
                em.wait("tensor", d_ident)
                em.wait("tensor", k_ready)
                last = None
                for oc in range(NDC):
                    last = em.emit("tensor", lambda e, o=b.ap[:, oc * 128:(oc + 1) * 128],
                                   i=k_grp[:, oc * 512 + tl * 128: oc * 512 + tl * 128 + 128]:
                                   e.transpose(o, i, ident_sb[:, :]),
                                   inc="pe" if oc == NDC - 1 else None)
                em.wait("vector", last)
                em.wait("vector", ktm_free)
                ev = em.emit("vector", lambda e,
                             o=ktm_grp[:, tl * 1024:(tl + 1) * 1024],
                             i=b.ap[:, :]: e.tensor_copy(o, i), inc="dve")
                b.free = [ev]
                ktm_ready.append([ev])
            ktm_free = []

            em.wait("vector", q_ready + k_ready)
            em.wait("vector", d_mask)
            for c4 in range(4):
                em.emit("vector", lambda e, o=mask4[:, c4 * 128:(c4 + 1) * 128],
                        i=mask_sb[:, :]: e.tensor_copy(o, i))

            bA = [fb[0], fb[1]]
            bO0 = [fb[2], fb[3]]
            bS = [fb[4], fb[5]]
            am_bufs = [(A_m, A_m2), (A_m_w1, A_m2_w1)]
            at_ready = [None] * 4
            attn_done_last = [None] * 4

            def emit_A6(cl):
                pe_last, evac = tm_tile_to_fm(attn_tm_slot[cl % 2], attn_fm, cl,
                                              None, attn_done_last[cl],
                                              bank=bS[cl % 2])
                attn_tm_free[cl % 2] = [pe_last]
                at_ready[cl] = evac

            for cl in range(4):
                if cl > 0:
                    emit_A6(cl - 1)
                wbanks = [(bA[0], bA[1]), (tb[0], tb[1])]
                waps = []
                ev_a = []
                for w in range(2):
                    b0, b1 = wbanks[w]
                    a0 = b0.ap if w == 0 else b0.ap.bitcast(F32)
                    a1 = b1.ap if w == 0 else b1.ap.bitcast(F32)
                    waps.append((a0, a1))
                    em.wait("tensor", b0.free + b1.free)
                    b0.free = []
                    b1.free = []
                    eve = evo = None
                    for jj in range(4):
                        j = 4 * w + jj
                        win = j * 512 + cl * 128
                        em.wait("tensor", [q_ready[j], k_ready[j]])
                        eve = em.emit("tensor", lambda e, o=a0[:, jj * 128:jj * 128 + 128],
                                      l=k_grp[0:64, win:win + 128],
                                      r=q_grp[0:64, win:win + 128],
                                      s=(jj == 0), p=(jj == 3):
                                      e.matmul(o, l, r, start=s, stop=p),
                                      inc="pe" if jj == 3 else None)
                        evo = em.emit("tensor", lambda e, o=a1[:, jj * 128:jj * 128 + 128],
                                      l=k_grp[64:128, win:win + 128],
                                      r=q_grp[64:128, win:win + 128],
                                      s=(jj == 0), p=(jj == 3):
                                      e.matmul(o, l, r, start=s, stop=p,
                                               tile_position=(64, 0)),
                                      inc="pe" if jj == 3 else None)
                    ev_a.append([eve, evo])

                ev_me = [None, None]
                ev_mo = [None, None]
                for w in range(2):
                    a0, a1 = waps[w]
                    ame, amo = am_bufs[w]
                    em.wait("vector", ev_a[w])
                    em.wait("vector", am_free[w])
                    ev_me[w] = em.emit("vector", lambda e, i=a0, o=ame:
                                       e.tensor_mul(o[:, :], i, mask4[:, :]), inc="dve")
                    ev_mo[w] = em.emit("vector", lambda e, i=a1, o=amo:
                                       e.tensor_mul(o[:, :], i, mask4[:, :]), inc="dve")
                    if w == 0:
                        bA[0].free = [ev_me[0]]
                        bA[1].free = [ev_mo[0]]

                for w in range(2):
                    ame, amo = am_bufs[w]
                    av_done = None
                    if w == 0:
                        bo0, bo1 = bO0[0], bO0[1]
                        o0, o1 = bo0.ap, bo1.ap
                    else:
                        bo0, bo1 = tb[0], tb[1]
                        o0, o1 = bo0.ap.bitcast(F32), bo1.ap.bitcast(F32)

                    def vap_of(h):
                        return vaug_grp[:, cl * 1056 + 66 * h: cl * 1056 + 66 * h + 66]

                    if w == 0:
                        em.wait("tensor", bo0.free + bo1.free)
                        bo0.free = []
                        bo1.free = []
                    em.wait("tensor", ev_me[w])
                    em.wait("tensor", v_ready[cl] + ktm_ready[cl])
                    for jj in range(4):
                        em.emit("tensor", lambda e, o=o0[:, 66 * jj:66 * jj + 66],
                                l=ame[:, jj * 128:jj * 128 + 128],
                                r=vap_of(2 * (4 * w + jj)), s=(jj == 0):
                                e.matmul(o, l, r, start=s, stop=False))
                    em.wait("tensor", ev_mo[w])
                    for jj in range(4):
                        av_done = em.emit("tensor", lambda e, o=o1[:, 66 * jj:66 * jj + 66],
                                l=amo[:, jj * 128:jj * 128 + 128],
                                r=vap_of(2 * (4 * w + jj) + 1), s=(jj == 0):
                                e.matmul(o, l, r, start=s, stop=False),
                                inc="pe" if jj == 3 else None)
                    am_free[w] = [av_done]
                    em.wait("tensor", mirror_ev[w] if mirror_ev[w] else d_const)
                    for jj in range(4):
                        j = 4 * w + jj
                        win = j * 512 + cl * 128
                        em.emit("tensor", lambda e, o=o0[:, 66 * jj:66 * jj + 66],
                                l=q_grp[0:64, win:win + 128],
                                r=S_bf[0:64, 66 * j:66 * j + 66], p=(jj == 3):
                                e.matmul(o, l, r, start=False, stop=p))
                    ev_qs = None
                    for jj in range(4):
                        j = 4 * w + jj
                        win = j * 512 + cl * 128
                        ev_qs = em.emit("tensor", lambda e, o=o1[:, 66 * jj:66 * jj + 66],
                                l=q_grp[64:128, win:win + 128],
                                r=S_bf[64:128, 66 * j:66 * j + 66], p=(jj == 3):
                                e.matmul(o, l, r, start=False, stop=p,
                                         tile_position=(64, 0)),
                                inc="pe" if jj == 3 else None)
                    em.wait("tensor", bS[0].free + bS[1].free)
                    bS[0].free = []
                    bS[1].free = []
                    pe_wave_last = None
                    for jj in range(4):
                        em.emit("tensor", lambda e, o=bS[0].ap[0:64, 66 * jj:66 * jj + 66],
                                l=ktm_grp[:, cl * 1024 + 128 * (4 * w + jj):
                                          cl * 1024 + 128 * (4 * w + jj) + 64],
                                r=vap_of(2 * (4 * w + jj)), s=(jj == 0), p=(jj == 3):
                                e.matmul(o, l, r, start=s, stop=p))
                        pe_wave_last = em.emit("tensor", lambda e,
                                o=bS[1].ap[0:64, 66 * jj:66 * jj + 66],
                                l=ktm_grp[:, cl * 1024 + 128 * (4 * w + jj) + 64:
                                          cl * 1024 + 128 * (4 * w + jj) + 128],
                                r=vap_of(2 * (4 * w + jj) + 1), s=(jj == 0), p=(jj == 3):
                                e.matmul(o, l, r, start=s, stop=p), inc="pe")
                    em.wait("vector", ev_qs)
                    for par, oo in ((0, o0), (1, o1)):
                        zin = oo[:, 0:264].rearrange("p (j c) -> p j c", c=66)[:, :, 64:65]
                        em.emit("vector", lambda e,
                                o=zr_sb[:, 8 * par:8 * par + 4].rearrange("p (j o) -> p j o", o=1),
                                i=zin: e.tensor_scalar_add(o, i, EPS_Z))
                    for par in range(2):
                        em.emit("vector", lambda e, o=zr_sb[:, 8 * par + 4:8 * par + 8],
                                i=zr_sb[:, 8 * par:8 * par + 4]: e.reciprocal(o, i))
                    em.wait("vector", zr_last[0])
                    zr_ev = em.emit("vector", lambda e: e.tensor_scalar_mul(
                        spc_a[:, 0:8], zr_sb[:, 0:8], 1.0), inc="dve")
                    zr_last[0] = zr_ev
                    em.wait("scalar", zr_ev)
                    em.wait("scalar", attn_tm_free[cl % 2])
                    evs0 = []
                    for jj in range(4):
                        h = 8 * w + 2 * jj
                        evv = em.emit("scalar", lambda e,
                                      o=attn_tm_slot[cl % 2][:, 64 * h: 64 * h + 64],
                                      i=o0[:, 66 * jj:66 * jj + 64],
                                      r=zr_sb[:, 4 + jj:5 + jj]:
                                      e.activation(o, i, AF.Copy, scale=r), inc="act")
                        evs0.append(evv)
                    bo0.free = evs0
                    em.wait("vector", attn_tm_free[cl % 2])
                    evs1 = []
                    for jj in range(4):
                        h = 8 * w + 2 * jj + 1
                        evv = em.emit("vector", lambda e,
                                      o=attn_tm_slot[cl % 2][:, 64 * h: 64 * h + 64],
                                      i=o1[:, 66 * jj:66 * jj + 64],
                                      r=zr_sb[:, 12 + jj:13 + jj]:
                                      e.tensor_scalar_mul(o, i, r), inc="dve")
                        evs1.append(evv)
                    bo1.free = evs1
                    attn_done_last[cl] = [evs0[-1], evs1[-1]]
                    scol = 264 * w
                    em.wait("vector", pe_wave_last)
                    em.emit("vector", lambda e, o=S_f32[0:64, scol:scol + 264],
                            i=bS[0].ap[0:64, 0:264]: e.tensor_add(o, o, i))
                    ev_su = em.emit("vector", lambda e, o=S_f32[64:128, scol:scol + 264],
                                    i=bS[1].ap[0:64, 0:264]: e.tensor_add(o, o, i), inc="dve")
                    bS[0].free = [ev_su]
                    bS[1].free = [ev_su]
                    em.wait("scalar", ev_su)
                    mirror_ev[w] = em.emit("scalar", lambda e, o=S_bf[:, scol:scol + 264],
                                           i=S_f32[:, scol:scol + 264]: e.copy(o, i), inc="act")
            qk_free = [em.pe_now()]
            scr_free[1] = scr_free[1] + [em.pe_now()]
            ktm_free = [em.pe_now()]
            vaug_free = [em.pe_now()]

            nxt = g + 1 < NG
            nready = []
            for tl in range(4):
                if tl == 2:
                    emit_A6(3)
                tt = 4 * g + tl
                for dh in range(2):
                    b = acquire_fb()
                    em.wait("tensor", [d_wo, d_brows])
                    em.wait("tensor", at_ready[tl])
                    for dc in range(NDC):
                        em.emit("tensor", lambda e, o=b.ap[:, :],
                                l=attn_fm[:, dc * 512 + tl * 128: dc * 512 + tl * 128 + 128],
                                r=wot_sb[:, dc * 1024 + dh * 512: dc * 1024 + dh * 512 + 512],
                                s=(dc == 0): e.matmul(o, l, r, start=s, stop=False))
                    last = em.emit("tensor", lambda e, o=b.ap[:, :],
                                   r=brows_sb[0:1, 2 * D + dh * 512: 2 * D + dh * 512 + 512]:
                                   e.matmul(o, ones_row[0:1, :], r, start=False, stop=True),
                                   inc="pe")
                    em.wait("vector", last)
                    ev = em.emit("vector", lambda e,
                                 o=x_tm[:, tt * D + dh * 512: tt * D + dh * 512 + 512],
                                 i=b.ap[:, :]:
                                 e.scalar_tensor_tensor(o, i, 1.0, o, op0=ALU.mult, op1=ALU.add),
                                 inc="dve")
                    b.free = [ev]
                    x_tile_y1[tt] = ev
                if nxt:
                    ln1_vs_tile(g + 1, tl)
                    if tl >= 1:
                        ln1_fin_tile(g + 1, tl - 1)
                    if tl >= 2:
                        nready.append(ln1_T_tile(g + 1, tl - 2))
                else:
                    if tl <= 2:
                        lnv2_a(tl)
                    if tl >= 2:
                        lnv2_b(tl - 2)
            if nxt:
                ln1_fin_tile(g + 1, 3)
                nready.append(ln1_T_tile(g + 1, 2))
                nready.append(ln1_T_tile(g + 1, 3))
                h1fm_ready = nready

        act_free = [[], []]
        sg_free = [[], []]
        act_evs_by_e = [dict() for _ in range(E)]
        x_final = [None] * NT

        def emit_gu(e_, tc):
            s = e_ % 2
            aslot = act_slot[s]
            for it in range(4):
                bg = acquire_fb()
                em.wait("tensor", exp_loads[e_])
                em.wait("tensor", [h2_ready[4 * tc + i] for i in range(4)])
                ev_g = None
                for dc in range(NDC):
                    ev_g = em.emit("tensor", lambda e, o=bg.ap[:, :],
                                   l=moe_slot[s]["g"][:, dc * 512 + it * 128: dc * 512 + it * 128 + 128],
                                   r=h2_fm[:, dc * 2048 + tc * 512: dc * 2048 + tc * 512 + 512],
                                   st=(dc == 0), sp=(dc == NDC - 1):
                                   e.matmul(o, l, r, start=st, stop=sp),
                                   inc="pe" if dc == NDC - 1 else None)
                bu = acquire_fb()
                ev_u = None
                for dc in range(NDC):
                    ev_u = em.emit("tensor", lambda e, o=bu.ap[:, :],
                                   l=moe_slot[s]["u"][:, dc * 512 + it * 128: dc * 512 + it * 128 + 128],
                                   r=h2_fm[:, dc * 2048 + tc * 512: dc * 2048 + tc * 512 + 512],
                                   st=(dc == 0), sp=(dc == NDC - 1):
                                   e.matmul(o, l, r, start=st, stop=sp),
                                   inc="pe" if dc == NDC - 1 else None)
                si = (tc * 4 + it) % 2
                em.wait("scalar", ev_g)
                em.wait("scalar", sg_free[si])
                em.wait("scalar", gp_a3_final)
                ev_sg = em.emit("scalar", lambda e, o=sg_scr[si], i=bg.ap[:, :]:
                                e.activation(o, i, AF.Silu), inc="act")
                bg.free = [ev_sg]
                em.wait("vector", [ev_u, ev_sg])
                em.wait("vector", act_free[s])
                ev_a = em.emit("vector", lambda e,
                               o=aslot[:, it * 2048 + tc * 512: it * 2048 + tc * 512 + 512],
                               i=bu.ap[:, :], sg=sg_scr[si]:
                               e.scalar_tensor_tensor(o, i, 1.0, sg, op0=ALU.mult, op1=ALU.mult),
                               inc="dve")
                bu.free = [ev_a]
                sg_free[si] = [ev_a]
                act_evs_by_e[e_][(tc, it)] = ev_a

        def emit_down(e_, comb_ev):
            s = e_ % 2
            aslot = act_slot[s]
            for tt in range(NT):
                tc = tt // 4
                for dh in range(2):
                    bd = acquire_fb()
                    em.wait("tensor", [act_evs_by_e[e_][(tc, it)] for it in range(4)])
                    last = None
                    for it in range(4):
                        last = em.emit("tensor", lambda e, o=bd.ap[:, :],
                                       l=aslot[:, it * 2048 + tt * 128: it * 2048 + tt * 128 + 128],
                                       r=moe_slot[s]["d"][:, it * 1024 + dh * 512: it * 1024 + dh * 512 + 512],
                                       st=(it == 0), sp=(it == 3):
                                       e.matmul(o, l, r, start=st, stop=sp),
                                       inc="pe" if it == 3 else None)
                    em.wait("vector", last)
                    em.wait("vector", comb_ev)
                    ev = em.emit("vector", lambda e,
                                 o=x_tm[:, tt * D + dh * 512: tt * D + dh * 512 + 512],
                                 i=bd.ap[:, :], c=combine[:, e_ * 16 + tt: e_ * 16 + tt + 1]:
                                 e.scalar_tensor_tensor(o, i, c, o, op0=ALU.mult, op1=ALU.add),
                                 inc="dve")
                    bd.free = [ev]
                    if e_ == E - 1:
                        x_final[tt] = ev
            act_free[s] = [em.pe_now()]
            slot_dead[s] = em.pe_now()
            if e_ + 2 < E:
                emit_exp_load(e_ + 2)

        def do_T2(tt):
            b = tb[tt % 2]
            em.wait("tensor", b.free)
            b.free = []
            em.wait("tensor", ev_s2[tt])
            last = None
            for dc in range(NDC):
                last = em.emit("tensor", lambda e, o=b.ap[:, dc * 128:(dc + 1) * 128],
                               i=h2_tm_slot[tt % 3][:, dc * 128:(dc + 1) * 128]:
                               e.transpose(o, i, ident_sb[:, :]),
                               inc="pe" if dc == NDC - 1 else None)
            h2_free[tt % 3] = [last]
            em.wait("vector", last)
            ev = em.emit("vector", lambda e,
                         o=h2_fm.rearrange("p (c t) -> p c t", c=NDC)[:, :, tt * 128:(tt + 1) * 128],
                         i=b.ap[:, :].rearrange("p (c w) -> p c w", c=NDC):
                         e.tensor_copy(o, i), inc="dve")
            h2_ready[tt] = ev
            b.free = [ev]

        def do_router(tt):
            brt = acquire_fb()
            em.wait("tensor", d_gwt)
            em.wait("tensor", h2_ready[tt])
            rlast = None
            for dc in range(NDC):
                rlast = em.emit("tensor", lambda e, o=brt.ap[:, 0:16],
                                l=h2_fm[:, dc * 2048 + tt * 128: dc * 2048 + tt * 128 + 128],
                                r=gwt_sb[:, dc * E:(dc + 1) * E],
                                s=(dc == 0), p=(dc == NDC - 1):
                                e.matmul(o, l, r, start=s, stop=p),
                                inc="pe" if dc == NDC - 1 else None)
            em.wait("scalar", rlast)
            evsig = em.emit("scalar", lambda e,
                            o=escore.rearrange("p (ee t) -> p ee t", ee=E)[:, :, tt:tt + 1],
                            i=brt.ap[:, 0:16].rearrange("p (a ee) -> p ee a", a=1):
                            e.activation(o, i, AF.Sigmoid), inc="act")
            sig_evs.append(evsig)
            brt.free = [evsig]

        for tc in range(4):
            for tl in range(4):
                tt = 4 * tc + tl
                do_T2(tt)
                if tt + 3 < NT:
                    lnv2_a(tt + 3)
                if tt + 2 < NT:
                    lnv2_b(tt + 2)
                if tt >= 1:
                    do_router(tt - 1)
            emit_gu(0, tc)
        do_router(NT - 1)

        def eb(t, e_):
            return t[:, e_ * 16:(e_ + 1) * 16]

        sfc3 = sfc.rearrange("p (g f) -> p g f", g=4)
        pt3 = ptree.rearrange("p (g f) -> p g f", g=4)
        spacer = lambda e: e.tensor_scalar_mul(spc, gs[:, 0:16], 1.0)
        comb_ev_box = [None]

        def emit_ge(d):
            w = (16 - d) * 16
            em.emit("vector", lambda e, o=cmpb[d % 3], x1=tmpsc[:, 0:w],
                    x2=tmpsc[:, d * 16:d * 16 + w], w=w:
                    e.tensor_tensor(o[:, 0:w], x1, x2, op=ALU.is_ge))

        def emit_rank_e(d_lo, d_hi):
            for d in range(d_lo, d_hi):
                w = (16 - d) * 16
                em.emit("vector", lambda e, x=rank_e[:, d * 16:d * 16 + w],
                        c=cmpb[d % 3][:, 0:w]: e.tensor_add(x, x, c))
                if d + 2 <= 15:
                    emit_ge(d + 2)
                else:
                    em.emit("vector", spacer)
                em.emit("vector", lambda e, x=rank_e[:, 0:w],
                        c=cmpb[d % 3][:, 0:w]: e.tensor_sub(x, x, c))
                em.emit("vector", spacer)

        def emit_tree_seg(seg):
            if seg == 0:
                em.wait("vector", sig_evs)
                em.wait("vector", d_ebb)
                for e_ in range(E):
                    em.emit("vector", lambda e, o=eb(sfc, e_), i=eb(escore, e_),
                            c=ebb_sb[:, e_:e_ + 1]: e.tensor_scalar_add(o, i, c))
                pairs4 = [(0, 1), (0, 2), (0, 3), (1, 2), (1, 3), (2, 3)]
                for pi, (a1, a2) in enumerate(pairs4):
                    em.emit("vector", lambda e, o=pt3[:, :, pi * 16:(pi + 1) * 16],
                            x1=sfc3[:, :, a1 * 16:(a1 + 1) * 16],
                            x2=sfc3[:, :, a2 * 16:(a2 + 1) * 16]:
                            e.tensor_add(o, x1, x2))
                for s1, s2, dst in ((0, 1, 6), (2, 3, 7), (4, 5, 8)):
                    em.emit("vector", lambda e, o=pt3[:, :, dst * 16:(dst + 1) * 16],
                            x1=pt3[:, :, s1 * 16:(s1 + 1) * 16],
                            x2=pt3[:, :, s2 * 16:(s2 + 1) * 16]:
                            e.tensor_max(o, x1, x2))
                em.emit("vector", spacer)
                em.emit("vector", lambda e, o=pt3[:, :, 144:160], x1=pt3[:, :, 96:112],
                        x2=pt3[:, :, 112:128]: e.tensor_max(o, x1, x2))
                em.emit("vector", spacer)
                em.emit("vector", lambda e, o=gs.rearrange("p (g f) -> p g f", g=4),
                        x1=pt3[:, :, 144:160], x2=pt3[:, :, 128:144]:
                        e.tensor_max(o, x1, x2))
                for gi in range(4):
                    em.emit("vector", lambda e, o=rank_g[:, gi * 16:(gi + 1) * 16],
                            v=float(3 - gi): e.memset(o, v))
            elif seg == 1:
                gcmp = [cmpb[i][:, 0:48] for i in range(3)]
                for d in range(1, 4):
                    w = (4 - d) * 16
                    em.emit("vector", lambda e, o=gcmp[d - 1][:, 0:w], x1=gs[:, 0:w],
                            x2=gs[:, d * 16:d * 16 + w], w=w:
                            e.tensor_tensor(o[:, 0:w], x1, x2, op=ALU.is_ge))
                for d in range(1, 4):
                    w = (4 - d) * 16
                    em.emit("vector", lambda e, x=rank_g[:, d * 16:d * 16 + w],
                            c=gcmp[d - 1][:, 0:w]: e.tensor_add(x, x, c))
                    em.emit("vector", spacer)
                    em.emit("vector", lambda e, x=rank_g[:, 0:w],
                            c=gcmp[d - 1][:, 0:w]: e.tensor_sub(x, x, c))
                    em.emit("vector", spacer)
                em.emit("vector", lambda e: e.tensor_single_scalar(gmask, rank_g, 2.0, ALU.is_lt))
                em.emit("vector", spacer)
                gm3 = gmask.rearrange("p (g f) -> p g f", g=4)
                tm3 = tmpsc.rearrange("p (g f) -> p g f", g=4)
                for a in range(4):
                    em.emit("vector", lambda e, o=tm3[:, :, a * 16:(a + 1) * 16],
                            x1=sfc3[:, :, a * 16:(a + 1) * 16], x2=gm3:
                            e.tensor_mul(o, x1, x2))
                for e_ in range(E):
                    em.emit("vector", lambda e, o=eb(rank_e, e_), v=float(15 - e_):
                            e.memset(o, v))
                emit_ge(1)
                emit_ge(2)
                emit_rank_e(1, 5)
            elif seg == 2:
                emit_rank_e(5, 11)
            else:
                emit_rank_e(11, 16)
                em.emit("vector", lambda e: e.tensor_single_scalar(sfc, rank_e, 4.0, ALU.is_lt))
                em.emit("vector", spacer)
                em.emit("vector", lambda e: e.tensor_mul(tmpsc, escore, sfc))
                em.emit("vector", spacer)
                tm8 = tmpsc.rearrange("p (h f) -> p h f", h=8)
                dt8 = dtree[:, 0:128].rearrange("p (h f) -> p h f", h=8)
                em.emit("vector", lambda e: e.tensor_add(dt8, tm8[:, :, 0:16], tm8[:, :, 16:32]))
                em.emit("vector", spacer)
                dt4i = dtree[:, 0:128].rearrange("p (h f) -> p h f", h=4)
                dt4 = dtree[:, 128:192].rearrange("p (h f) -> p h f", h=4)
                em.emit("vector", lambda e: e.tensor_add(dt4, dt4i[:, :, 0:16], dt4i[:, :, 16:32]))
                em.emit("vector", spacer)
                dt2i = dtree[:, 128:192].rearrange("p (h f) -> p h f", h=2)
                em.emit("vector", lambda e, o=dtree[:, 192:224].rearrange("p (h f) -> p h f", h=2):
                        e.tensor_add(o, dt2i[:, :, 0:16], dt2i[:, :, 16:32]))
                em.emit("vector", spacer)
                em.emit("vector", lambda e: e.tensor_add(den, dtree[:, 192:208], dtree[:, 208:224]))
                em.emit("vector", spacer)
                em.emit("vector", lambda e: e.tensor_scalar_add(den, den, 1e-20))
                em.emit("vector", spacer)
                em.emit("vector", lambda e: e.reciprocal(rden, den))
                em.emit("vector", spacer)
                for e_ in range(E):
                    comb_ev_box[0] = em.emit("vector",
                                             lambda e, o=eb(combine, e_), i=eb(tmpsc, e_):
                                             e.tensor_mul(o, i, rden), inc="dve")

        for tc in range(4):
            emit_gu(1, tc)
            emit_tree_seg(tc)
        for k in range(E // 2):
            if k > 0:
                for tc in range(4):
                    emit_gu(2 * k, tc)
                for tc in range(4):
                    emit_gu(2 * k + 1, tc)
            emit_down(2 * k, comb_ev_box[0])
            emit_down(2 * k + 1, comb_ev_box[0])

        yv = y_d.rearrange("(tt p) d -> p tt d", p=128)
        for tt in range(NT):
            qeng = "sync" if tt % 2 == 0 else "gpsimd"
            em.wait(qeng, x_final[tt])
            em.emit(qeng, lambda e, o=yv[:, tt:tt + 1, :],
                    i=x_tm[:, tt * D:(tt + 1) * D].rearrange("p (a d) -> p a d", a=1):
                    e.dma_start(out=o, in_=i), inc="st", n=16)
        em.wait("sync", ("st", em.ctr["st"]))

        em.check_deadlock()

        with nc.Block() as block:
            @block.tensor
            def _(t):
                for f in em.ops["tensor"]:
                    f(t)

            @block.vector
            def _(v):
                for f in em.ops["vector"]:
                    f(v)

            @block.scalar
            def _(sx):
                for f in em.ops["scalar"]:
                    f(sx)

            @block.gpsimd
            def _(gp):
                for f in em.ops["gpsimd"]:
                    f(gp)

            @block.sync
            def _(sy):
                for f in em.ops["sync"]:
                    f(sy)

    return nc


_NC_CACHE = None


def kernel(L0_attn_x, L0_attn_Wq, L0_attn_bq, L0_attn_Wk, L0_attn_bk,
           L0_attn_Wv, L0_attn_bv, L0_attn_Wo, L0_attn_bo,
           L0_ffn_gate_weight, L0_ffn_e_bias, L0_ffn_gate_proj,
           L0_ffn_up_proj, L0_ffn_down_proj):
    global _NC_CACHE
    x = np.ascontiguousarray(np.asarray(L0_attn_x, dtype=np.float32))
    f32 = lambda a: np.ascontiguousarray(np.asarray(a, dtype=np.float32))

    wqt = f32(np.asarray(L0_attn_Wq).T)
    wkt = f32(np.asarray(L0_attn_Wk).T)
    wvt = f32(np.asarray(L0_attn_Wv).T)
    wot = f32(np.asarray(L0_attn_Wo).T)
    gwt = f32(np.asarray(L0_ffn_gate_weight).T)
    gT = f32(np.asarray(L0_ffn_gate_proj).transpose(0, 2, 1))
    uT = f32(np.asarray(L0_ffn_up_proj).transpose(0, 2, 1))
    dT = f32(np.asarray(L0_ffn_down_proj).transpose(0, 2, 1))

    bq = f32(L0_attn_bq)
    bk = f32(L0_attn_bk)
    bv = f32(L0_attn_bv)
    bo = f32(L0_attn_bo)
    bias_cm = np.concatenate([bq.reshape(8, 128).T, bk.reshape(8, 128).T],
                             axis=1)
    bias_cm = f32(bias_cm)
    brows = f32(np.concatenate([bk, bv, bo]))
    ebb = f32(np.broadcast_to(np.asarray(L0_ffn_e_bias, np.float32)[None, :], (128, E)))
    mask = f32(np.triu(np.ones((128, 128), np.float32)))
    ident = f32(np.eye(128, dtype=np.float32))

    if _NC_CACHE is None:
        _NC_CACHE = build_program()
    nc = _NC_CACHE

    shared = dict(wqt=wqt, wkt=wkt, wvt=wvt, wot=wot, gwt=gwt,
                  gateT=gT, upT=uT, downT=dT, bias_cm=bias_cm, brows=brows,
                  ebb=ebb, mask=mask, ident=ident)
    in_maps = [dict(shared, x=np.ascontiguousarray(x[i])) for i in range(8)]
    res = run_bass_kernel_spmd(nc, in_maps, core_ids=list(range(8)))
    out = np.stack([res.results[i]["y"] for i in range(8)], axis=0)
    return out.astype(np.float32)
